# version 03
# baseline (speedup 1.0000x reference)
import numpy as np
import ml_dtypes
from contextlib import ExitStack
import concourse.bass as bass
import concourse.mybir as mybir
from concourse.bass_utils import run_bass_kernel_spmd

F32 = mybir.dt.float32
BF16 = mybir.dt.bfloat16
AX = mybir.AxisListType.X
OP = mybir.AluOpType
AF = mybir.ActivationFunctionType
bf = ml_dtypes.bfloat16

D = 4096
TOK = 1024
NT = 8
NG = 4
NCD = D // 128
MEXP = 12.0
NEG = -1.0e30


def nchpad_tile(j):
    return 7 + 8 * (j // 2) + (j % 2)


def nchpad_group(g):
    return 8 + 8 * g


class Prog:
    ENGS = ("tensor", "vector", "scalar", "gpsimd", "sync")

    def __init__(self, nc, es):
        self.nc = nc
        self.es = es
        self.seq = {e: [] for e in self.ENGS}
        self.sems = {}
        self.cnt = {}

    def sem(self, name):
        if name not in self.sems:
            self.sems[name] = self.es.enter_context(self.nc.semaphore(name))
            self.cnt[name] = 0
        return self.sems[name]

    def c(self, name):
        return self.cnt.get(name, 0)

    def emit(self, eng, fn, waits=(), inc=None, incn=1):
        wv = []
        for w in waits:
            if isinstance(w, tuple):
                s, v = w
            else:
                s, v = w, self.c(w)
            self.sem(s)
            if v > 0:
                wv.append((s, v))
        tgt = None
        if inc is not None:
            self.sem(inc)
            self.cnt[inc] = self.c(inc) + incn
            tgt = (inc, incn)
        self.seq[eng].append((fn, wv, tgt))

    def dma(self, eng, out, in_, waits=(), inc="ld", **kw):
        self.emit(eng, lambda e: e.dma_start(out=out, in_=in_, **kw),
                  waits=waits, inc=inc, incn=16)

    def build(self):
        nc = self.nc
        blk = self.es.enter_context(nc.Block())
        for eng_name in self.ENGS:
            seq = self.seq[eng_name]

            def body(e, seq=seq):
                lastw = {}
                for fn, wv, tgt in seq:
                    for s, v in wv:
                        if lastw.get(s, 0) < v:
                            e.wait_ge(self.sems[s], v)
                            lastw[s] = v
                    ins = fn(e)
                    if tgt is not None:
                        ins.then_inc(self.sems[tgt[0]], tgt[1])
            getattr(blk, eng_name)(body)


def new_nc():
    return bass.Bass("TRN2", target_bir_lowering=False, debug=False,
                     detect_race_conditions=False)


RG = [[0, 1, 2, 3], [4, 5, 6, 7]]
DBG_DUMP = False


class Arena:
    def __init__(self, nc, tag, base):
        self.nc, self.tag, self.base, self.addr = nc, tag, base, 0

    def sb(self, name, shape, dtype):
        per = int(np.prod(shape[1:])) * mybir.dt.size(dtype)
        a = (self.addr + 31) & ~31
        self.addr = a + per
        assert self.base + self.addr <= 229375, (self.tag, self.addr)
        return self.nc.alloc_sbuf_tensor_at(
            f"{name}_{self.tag}", shape, dtype, offset=self.base + a)


def barrier(p, tag="ph"):
    snap = [(s, p.c(s)) for s in list(p.cnt.keys()) if p.c(s) > 0 and s != "cc"]
    for eng in Prog.ENGS:
        p.emit(eng, lambda e: e.nop(), waits=snap, inc=tag)
    n = p.c(tag)
    for eng in Prog.ENGS:
        p.emit(eng, lambda e: e.nop(), waits=[(tag, n)])


def emit_cc(p, src_t, dst_t, waits):
    p.emit("gpsimd", lambda g_: g_.collective_compute(
        "AllGather", OP.bypass, replica_groups=RG,
        ins=[src_t.ap().opt()], outs=[dst_t.ap().opt()]), waits=waits, inc="cc")


def emit_ln(p, x_sb, xn_sb, stats, junk, eps=1e-5, dim=D):
    p.emit("vector", lambda v: v.reduce_sum(stats[:, 0:1], x_sb, AX), inc="vc")
    p.emit("scalar", lambda s: s.activation(junk, x_sb, AF.Square,
                                            accum_out=stats[:, 1:2]),
           waits=[("vc", p.c("vc"))], inc="sc")

    def _m(v):
        v.tensor_scalar(stats[:, 0:1], stats[:, 0:1], 1.0 / dim, None, OP.mult)
        v.scalar_tensor_tensor(stats[:, 2:3], stats[:, 0:1], 1.0, stats[:, 0:1],
                               OP.mult, OP.mult)
        v.tensor_scalar(stats[:, 1:2], stats[:, 1:2], 1.0 / dim, None, OP.mult)
        v.tensor_tensor(stats[:, 1:2], stats[:, 1:2], stats[:, 2:3], OP.subtract)
        return v.tensor_scalar(stats[:, 1:2], stats[:, 1:2], eps, None, OP.add)
    p.emit("vector", _m, waits=[("sc", p.c("sc"))], inc="vc")
    p.emit("scalar", lambda s: s.activation(stats[:, 3:4], stats[:, 1:2], AF.Ln),
           waits=("vc",), inc="sc")
    p.emit("scalar", lambda s: s.activation(stats[:, 1:2], stats[:, 3:4], AF.Exp,
                                            scale=-0.5), inc="sc")
    p.emit("vector", lambda v: v.scalar_tensor_tensor(
        stats[:, 0:1], stats[:, 0:1], -1.0, stats[:, 1:2], OP.mult, OP.bypass),
        waits=("sc",), inc="vc")
    p.emit("vector", lambda v: v.tensor_scalar(
        xn_sb, x_sb, stats[:, 0:1], stats[:, 1:2], OP.add, OP.mult),
        waits=[("txp", p.c("txp")), ("vc", p.c("vc"))], inc="sc")




def emit_transpose_tile(p, src_tm_fn, dst_fm_fn, identb, banks, nch, extra_waits=(),
                        rows_fn=None):
    base = p.c("tvx")
    for c in range(nch):
        bank = banks[(base + c) % 2]
        rows = 128 if rows_fn is None else rows_fn(c)
        w = list(extra_waits)
        if base + c - 1 > 0:
            w.append(("tvx", base + c - 1))
        p.emit("tensor", lambda t, c=c, bank=bank, rows=rows: t.transpose(
            bank[0:rows, 0:128], src_tm_fn(c), identb), waits=w, inc="txp")
        p.emit("vector", lambda v, c=c, bank=bank, rows=rows: v.tensor_copy(
            dst_fm_fn(c), bank[0:rows, 0:128]), waits=("txp",), inc="tvx")


def emit_rope_block(p, pa, j, nheads, cossb, sinsb, rtmp, rtmp2, outb):
    n = nheads
    cosv = cossb[:, j * 32:(j + 1) * 32].unsqueeze(1).to_broadcast([128, n, 32])
    sinv = sinsb[:, j * 32:(j + 1) * 32].unsqueeze(1).to_broadcast([128, n, 32])
    src = pa[:, 0:n * 64].rearrange("p (h r two) -> p h r two", h=n, two=2)
    dst = outb[:, 0:n * 64].rearrange("p (h r two) -> p h r two", h=n, two=2)
    t1 = rtmp[:, 0:n * 32].rearrange("p (h r) -> p h r", h=n)
    t2 = rtmp2[:, 0:n * 32].rearrange("p (h r) -> p h r", h=n)

    def _r(v):
        v.tensor_tensor(t1, src[:, :, :, 0], cosv, OP.mult)
        v.tensor_tensor(t2, src[:, :, :, 1], sinv, OP.mult)
        v.tensor_tensor(dst[:, :, :, 0], t1, t2, OP.subtract)
        v.tensor_tensor(t1, src[:, :, :, 1], cosv, OP.mult)
        v.tensor_tensor(t2, src[:, :, :, 0], sinv, OP.mult)
        return v.tensor_tensor(dst[:, :, :, 1], t1, t2, OP.add)
    p.emit("vector", _r, waits=[("tc", p.c("tc")), ("stx", p.c("stx"))], inc="vpe")


def emit_phase1(nc, es, p, A, X):
    x0 = X["x0"]; wqkv = X["wqkv"]; bqkv = X["bqkv"]; identb_d = X["identb_d"]
    xn0fm_d = X["xn0fm_d"]; q0fm_d = X["q0fm_d"]
    k0fm_d = X["k0fm_d"]; v0tm_d = X["v0tm_d"]

    xnfm = A.sb("xnfm", [128, NCD * TOK], BF16)
    xtile = A.sb("xtile", [128, D], F32)
    xntile = A.sb("xntile", [128, D], BF16)
    junk = A.sb("junk", [128, D], BF16)
    stats = A.sb("stats", [128, 8], F32)
    wbuf = A.sb("wbuf", [128, NCD * 512], BF16)
    brow = A.sb("brow", [1, 3072], BF16)
    ones1 = A.sb("ones1", [1, 512], BF16)
    identb = A.sb("identb_sb", [128, 128], BF16)
    stage = A.sb("stage", [128, 512], BF16)
    pst1 = X["pst1"]; pst2 = X["pst2"]; pa = X["ps"][0]; pb = X["ps"][1]

    p.dma("gpsimd", identb[:, :], identb_d)
    p.dma("gpsimd", brow[:, :], bqkv)
    p.emit("vector", lambda v: v.memset(ones1[:, :], 1.0), inc="vc")

    for j in range(NT):
        p.dma("gpsimd", xtile[:, :], x0[j * 128:(j + 1) * 128, :],
              waits=[("sc", p.c("sc"))], inc="ldx")
        p.emit("vector", lambda v: v.tensor_copy(stats[:, 4:5], stats[:, 4:5]) if False else v.memset(stats[:, 4:5], 0.0),
               waits=[("ldx", p.c("ldx"))], inc="vc")
        emit_ln(p, xtile[:, :], xntile[:, :], stats, junk[:, :])
        emit_transpose_tile(
            p, lambda c: xntile[:, c * 128:(c + 1) * 128],
            lambda c, j=j: xnfm[:, c * TOK + j * 128: c * TOK + j * 128 + 128],
            identb[:, :], (pst1, pst2), NCD, extra_waits=[("sc", p.c("sc"))])
    for c in range(NCD):
        p.dma("sync", xn0fm_d[c * 128:(c + 1) * 128, :], xnfm[:, c * TOK:(c + 1) * TOK],
              waits=[("tvx", p.c("tvx"))], inc="st")

    pss = [pa, pb]
    pi = [0]

    def fm_out_block(col0, out_d, row0):
        p.dma("gpsimd", wbuf[:, :].rearrange("p (c n) -> p c n", c=NCD),
              wqkv[:, col0:col0 + 512].rearrange("(c p) n -> p c n", p=128),
              waits=[("tc", p.c("tc"))], inc="ldw")
        for fsub in range(4):
            for st_ in range(2):
                ps = pss[pi[0] % 2]
                pi[0] += 1

                def _mm(t, ps=ps, fsub=fsub, st_=st_, col0=col0):
                    t.matmul(ps[:, 0:512], brow[0:1, col0 + fsub * 128: col0 + fsub * 128 + 128],
                             ones1[0:1, :], start=True, stop=False)
                    ins = None
                    for c in range(NCD):
                        ins = t.matmul(ps[:, 0:512],
                                       wbuf[:, c * 512 + fsub * 128: c * 512 + fsub * 128 + 128],
                                       xnfm[:, c * TOK + st_ * 512: c * TOK + st_ * 512 + 512],
                                       start=False, stop=(c == NCD - 1))
                    return ins
                p.emit("tensor", _mm,
                       waits=[("ldw", p.c("ldw")), ("tvx", p.c("tvx")),
                              ("ld", p.c("ld")), ("vc", 1), ("vpe", p.c("vpe"))],
                       inc="tc")
                p.emit("vector", lambda v, ps=ps: v.tensor_copy(stage[:, :], ps[:, 0:512]),
                       waits=[("tc", p.c("tc")), ("st", p.c("st"))], inc="vpe")
                p.dma("sync", out_d[col0 - row0 + fsub * 128: col0 - row0 + fsub * 128 + 128,
                                    st_ * 512:(st_ + 1) * 512],
                      stage[:, :], waits=[("vpe", p.c("vpe"))], inc="st")

    def tm_out_block(col0, out_d, ocol0):
        p.dma("gpsimd", wbuf[:, :].rearrange("p (c n) -> p c n", c=NCD),
              wqkv[:, col0:col0 + 512].rearrange("(c p) n -> p c n", p=128),
              waits=[("tc", p.c("tc"))], inc="ldw")
        for j in range(NT):
            ps = pss[pi[0] % 2]
            pi[0] += 1

            def _mm(t, ps=ps, j=j, col0=col0):
                t.matmul(ps[:, 0:512], ones1[0:1, 0:128], brow[0:1, col0:col0 + 512],
                         start=True, stop=False)
                ins = None
                for c in range(NCD):
                    ins = t.matmul(ps[:, 0:512],
                                   xnfm[:, c * TOK + j * 128: c * TOK + j * 128 + 128],
                                   wbuf[:, c * 512:(c + 1) * 512],
                                   start=False, stop=(c == NCD - 1))
                return ins
            p.emit("tensor", _mm,
                   waits=[("ldw", p.c("ldw")), ("tvx", p.c("tvx")),
                          ("ld", p.c("ld")), ("vc", 1), ("vpe", p.c("vpe"))],
                   inc="tc")
            p.emit("vector", lambda v, ps=ps: v.tensor_copy(stage[:, :], ps[:, 0:512]),
                   waits=[("tc", p.c("tc")), ("st", p.c("st"))], inc="vpe")
            p.dma("sync", out_d[j * 128:(j + 1) * 128, col0 - ocol0: col0 - ocol0 + 512],
                  stage[:, :], waits=[("vpe", p.c("vpe"))], inc="st")

    for blk in range(2):
        fm_out_block(blk * 512, q0fm_d, 0)
    for blk in range(2):
        fm_out_block(1024 + blk * 512, k0fm_d, 1024)
    for blk in range(2):
        tm_out_block(2048 + blk * 512, v0tm_d, 2048)



SC0 = 1.0 / 32.0


def emit_phase2(nc, es, p, A, X):
    x0 = X["x0"]; wo = X["wo"]; bo = X["bo"]; kern = X["kern"]; bias0 = X["bias0"]
    acomb = X["acomb"]; bcomb = X["bcomb"]; gate_sel = X["gate_sel"]
    qa = X["qa"]; qbn = X["qbn"]; qbp = X["qbp"]; wq_row = X["wq_row"]
    kva = X["kva"]; kvbk = X["kvbk"]; kvbv = X["kvbv"]; wkv_row = X["wkv_row"]
    combow = X["combow"]; idx_wq_d = X["idxwq"]; idx_wk_d = X["idxwk"]
    bthr = X["bthr"]; wbaset = X["wbaset"]; bbase = X["bbase"]
    acat = X["acat"]; bcat_d = X["bcat"]; ropec = X["ropec"]; ropes = X["ropes"]
    xn0fm_d = X["xn0fm_d"]; identb_d = X["identb_d"]
    x2p_d = X["x2p_d"]; q1nfm_d = X["q1nfm_d"]; q1pe_d = X["q1pe_d"]
    k1nfm_d = X["k1nfm_d"]; k1pe_d = X["k1pe_d"]; v1tm_d = X["v1tm_d"]
    ikfm_d = X["ikfm_d"]; iqfm_d = X["iqfm_d"]; iw_d = X["iw_d"]
    x1part = X["x1part"]; ex_k0 = X["ex_k0"]; ex_v0 = X["ex_v0"]
    q0fm_d = X["q0fm_d"]
    dbg_qlat = dbg_combo = dbg_hgfm = dbg_latfm = dbg_wn = dbg_st = dbg_h = None

    arena = A.sb("arena", [128, 32768], BF16)
    wbuf = A.sb("wbuf", [128, 16384], BF16)
    r3 = A.sb("r3", [128, 16384], BF16)
    r4 = A.sb("r4", [128, 12800], BF16)
    r5 = A.sb("r5", [128, 8192], BF16)
    stage = A.sb("stage", [128, 512], F32)
    stage2 = A.sb("stage2", [128, 512], F32)
    stageb = A.sb("stageb", [128, 512], BF16)
    hgbuf = A.sb("hgbuf", [128, 128], BF16)
    stats = A.sb("stats", [128, 16], F32)
    brow2 = A.sb("brow2", [1, 512], BF16)
    rows = A.sb("rows", [1, 2048], BF16)
    rowsf = A.sb("rowsf", [1, 512], F32)
    ones1 = A.sb("ones1", [1, 512], BF16)
    onesf = A.sb("onesf", [1, 128], F32)
    onescol = A.sb("onescol", [128, 1], BF16)
    identb = A.sb("identb_sb", [128, 128], BF16)
    cossb = A.sb("cossb", [128, 256], F32)
    sinsb = A.sb("sinsb", [128, 256], F32)
    gb = A.sb("gb", [128, 128], F32)
    wqb = A.sb("wqb", [128, 1536], BF16)
    wkvb = A.sb("wkvb", [128, 512], BF16)
    bthrb = A.sb("bthrb", [128, 1], F32)
    mexpb = A.sb("mexpb", [128, 1], F32)
    combo = A.sb("combo", [128, 8 * 32], F32)
    gatew = A.sb("gatew", [128, 16], F32)
    hgfm = A.sb("hgfm", [128, 8 * 128], BF16)
    bcatb = A.sb("bcatb", [128, 4096], BF16)
    abuf2 = A.sb("abuf2", [128, 32 * 32], BF16)
    bbuf2 = A.sb("bbuf2", [32, 512], BF16)
    rtmp = A.sb("rtmp", [128, 256], F32)
    rtmp2 = A.sb("rtmp2", [128, 256], F32)

    ps = X["ps"]; pst1 = X["pst1"]; pst2 = X["pst2"]
    psum1 = X["psum1"]; psm = X["psm"]

    xn0fm = arena
    v0 = arena
    xn1fm = arena
    r3f = r3.bitcast(F32)
    r4f = r4.bitcast(F32)
    g = "gpsimd"

    p.dma(g, identb[:, :], identb_d)
    p.dma(g, cossb[:, :].rearrange("p (c n) -> p c n", c=8),
          ropec.rearrange("(c p) n -> p c n", p=128))
    p.dma(g, sinsb[:, :].rearrange("p (c n) -> p c n", c=8),
          ropes.rearrange("(c p) n -> p c n", p=128))
    p.dma(g, rows[0:1, 0:128], gate_sel)
    p.dma(g, rows[0:1, 128:128 + 1536], wq_row)
    p.dma(g, rowsf[0:1, 0:1], bthr)
    p.emit("vector", lambda v: v.memset(ones1[:, :], 1.0), inc="vc")
    p.emit("vector", lambda v: v.memset(onesf[:, :], 1.0), inc="vc")
    p.emit("vector", lambda v: v.memset(onescol[:, :], 1.0), inc="vc")
    p.emit("vector", lambda v: v.memset(mexpb[:, :], -MEXP), inc="vc")

    def bc128(dst, src_row, n, f32src, bank=psm):
        p.emit("tensor", lambda t: t.matmul(
            bank[:, 0:n], (onesf if f32src else ones1)[0:1, 0:128], src_row,
            start=True, stop=True),
            waits=[("ld", p.c("ld")), ("vc", p.c("vc")), ("vpe", p.c("vpe"))], inc="tc")
        p.emit("vector", lambda v: v.tensor_copy(dst, bank[:, 0:n]),
               waits=[("tc", p.c("tc"))], inc="vpe")

    bc128(gb[:, :], rows[0:1, 0:128], 128, False)
    for i in range(3):
        bc128(wqb[:, i * 512:(i + 1) * 512], rows[0:1, 128 + i * 512:128 + (i + 1) * 512], 512, False)
    bc128(bthrb[:, :], rowsf[0:1, 0:1], 1, True)

    p.dma(g, brow2[0:1, 0:512], wkv_row, waits=[("vpe", p.c("vpe"))], inc="ld")
    bc128(wkvb[:, :], brow2[0:1, 0:512], 512, False)

    p.dma(g, xn0fm[:, :].rearrange("p (c n) -> p c n", c=NCD),
          xn0fm_d.rearrange("(c p) n -> p c n", p=128), inc="ldA")

    p.dma(g, wbuf[:, 0:NCD * 512].rearrange("p (c n) -> p c n", c=NCD),
          acomb[:, 0:512].rearrange("(c p) n -> p c n", p=128), inc="ldw")
    for q_ in range(4):
        p.dma(g, abuf2[:, q_ * 256:(q_ + 1) * 256].rearrange("p (c n) -> p c n", c=8),
              acomb[q_ * 1024:(q_ + 1) * 1024, 512:544].rearrange("(c p) n -> p c n", p=128),
              inc="ldw")
    axfm = r4

    for j in range(NT):
        pa = ps[j % 2]

        def _mm(t, j=j, pa=pa):
            ins = None
            for c in range(NCD):
                ins = t.matmul(pa[:, 0:512], xn0fm[:, c * TOK + j * 128: c * TOK + j * 128 + 128],
                               wbuf[:, c * 512:(c + 1) * 512], start=(c == 0), stop=(c == NCD - 1))
            return ins
        p.emit("tensor", _mm, waits=[("ldw", p.c("ldw")), ("ldA", p.c("ldA")),
                                     ("vpe", p.c("vpe")), ("vc", p.c("vc"))], inc="tc")
        def _ev(v, j=j, pa=pa):
            ins = None
            for n in range(16):
                ins = v.tensor_scalar(r3[:, n * 32:(n + 1) * 32], pa[:, n * 32:(n + 1) * 32],
                                      gb[:, j * 16 + n: j * 16 + n + 1], None, OP.mult)
            return ins
        p.emit("vector", _ev, waits=[("tc", p.c("tc")), ("txp", p.c("txp"))], inc="vpe")
        pb = ps[2 + j % 2]

        def _mm2(t, j=j, pb=pb):
            ins = None
            for c in range(NCD):
                ins = t.matmul(pb[:, 0:32], xn0fm[:, c * TOK + j * 128: c * TOK + j * 128 + 128],
                               abuf2[:, c * 32:(c + 1) * 32], start=(c == 0), stop=(c == NCD - 1))
            return ins
        p.emit("tensor", _mm2, waits=[("ldw", p.c("ldw")), ("vpe2", p.c("vpe2"))], inc="tc")
        p.emit("vector", lambda v, pb=pb: v.tensor_copy(r3[:, 512:544], pb[:, 0:32]),
               waits=[("tc", p.c("tc")), ("txp", p.c("txp"))], inc="vpe2")
        emit_transpose_tile(
            p, lambda c: r3[:, c * 128:c * 128 + (128 if c < 4 else 32)],
            lambda c, j=j: (axfm[:, j * 640 + c * 128: j * 640 + c * 128 + 128] if c < 4
                            else axfm[0:32, j * 640 + 512: j * 640 + 640]),
            identb[:, :], (pst1, pst2), 5,
            extra_waits=[("vpe", p.c("vpe")), ("vpe2", p.c("vpe2"))],
            rows_fn=lambda c: 128 if c < 4 else 32)

    for ob in range(8):
        p.dma(g, wbuf[:, :].rearrange("p (c n) -> p c n", c=NCD),
              kern[:, ob * 512:(ob + 1) * 512].rearrange("(c p) n -> p c n", p=128),
              waits=[("tc", p.c("tc"))], inc="ldw")
        p.dma(g, brow2[0:1, :], bias0[0:1, ob * 512:(ob + 1) * 512],
              waits=[("tc", p.c("tc"))], inc="ldw")
        p.dma(g, r5[:, 0:2048].rearrange("p (c n) -> p c n", c=4),
              bcomb[0:512, ob * 512:(ob + 1) * 512].rearrange("(c p) n -> p c n", p=128),
              waits=[("tc", p.c("tc"))], inc="ldw")
        p.dma(g, bbuf2[:, :], bcomb[512:544, ob * 512:(ob + 1) * 512],
              waits=[("tc", p.c("tc"))], inc="ldw")
        for j in range(NT):
            pa = ps[j % 2]
            p.dma("sync", stage2[:, :], x0[j * 128:(j + 1) * 128, ob * 512:(ob + 1) * 512],
                  waits=[("vpe", p.c("vpe"))], inc="ldx")

            def _mm(t, j=j, pa=pa):
                t.matmul(pa[:, 0:512], ones1[0:1, 0:128], brow2[0:1, :], start=True, stop=False)
                for c in range(NCD):
                    t.matmul(pa[:, 0:512], xn0fm[:, c * TOK + j * 128: c * TOK + j * 128 + 128],
                             wbuf[:, c * 512:(c + 1) * 512], start=False, stop=False)
                for c in range(4):
                    t.matmul(pa[:, 0:512], axfm[:, j * 640 + c * 128: j * 640 + c * 128 + 128],
                             r5[:, c * 512:(c + 1) * 512], start=False, stop=False)
                return t.matmul(pa[:, 0:512], axfm[0:32, j * 640 + 512: j * 640 + 640],
                                bbuf2[:, :], start=False, stop=True)
            p.emit("tensor", _mm, waits=[("ldw", p.c("ldw")), ("tvx", p.c("tvx")),
                                         ("vpe", p.c("vpe"))], inc="tc")
            p.emit("vector", lambda v, pa=pa: v.scalar_tensor_tensor(
                stage[:, :], pa[:, 0:512], 1.0, stage2[:, :], OP.mult, OP.add),
                waits=[("tc", p.c("tc")), ("ldx", p.c("ldx")), ("stx", p.c("stx"))], inc="vpe")
            p.dma("sync", x1part[j * 128:(j + 1) * 128, ob * 512:(ob + 1) * 512], stage[:, :],
                  waits=[("vpe", p.c("vpe"))], inc="stx")

    for c_ in range(32):
        m_, u0_ = (c_ // 2) % 4, 256 * (c_ // 8) + (c_ % 2) * 128
        p.dma(g, v0[:, c_ * 1024:(c_ + 1) * 1024],
              ex_v0[m_ * 1024 + u0_: m_ * 1024 + u0_ + 128, :],
              waits=[("tc", p.c("tc")), ("cc", 2)], inc="ldA")
    p.dma(g, r4[:, 0:8192].rearrange("p (h t) -> p h t", h=8),
          q0fm_d.rearrange("(h p) t -> p h t", p=128),
          waits=[("tc", p.c("tc"))], inc="ldA")
    pt = r5
    attn0fm = r4
    recipb = r4f
    kbuf = r4

    for gi in range(NG):
        for kc in range(32):
            kb0 = 10752 + (kc % 2) * 1024
            m_, u0_ = (kc // 2) % 4, 256 * (kc // 8) + (kc % 2) * 128
            p.dma("sync", kbuf[:, kb0:kb0 + 1024].rearrange("p (c n) -> p c n", c=8),
                  X["ex_k0"][m_ * 1024: (m_ + 1) * 1024, u0_:u0_ + 128].rearrange(
                      "(h p) u -> p h u", p=128),
                  waits=[("tc", p.c("tc") - 1), ("cc", 2)], inc="ldk")
            pa = ps[kc % 2]

            def _mm(t, gi=gi, kc=kc, pa=pa, kb0=kb0):
                ins = None
                for c in range(8):
                    ins = t.matmul(pa[:, 0:256], kbuf[:, kb0 + c * 128: kb0 + c * 128 + 128],
                                   r4[:, c * TOK + gi * 256: c * TOK + gi * 256 + 256],
                                   start=(c == 0), stop=(c == 7))
                return ins
            p.emit("tensor", _mm, waits=[("ldk", p.c("ldk")), ("ldA", p.c("ldA")),
                                         ("scA", p.c("scA") - 1)], inc="tc")
            p.emit("scalar", lambda s, kc=kc, pa=pa: s.activation(
                pt[:, kc * 256:(kc + 1) * 256], pa[:, 0:256], AF.Exp,
                bias=mexpb[:, 0:1], scale=SC0),
                waits=[("tc", p.c("tc")), ("vc", 4)], inc="scA")
        def _sums(t):
            ins = None
            for kc in range(32):
                ins = t.matmul(psum1[0:1, 0:256], onescol[:, :], pt[:, kc * 256:(kc + 1) * 256],
                               start=(kc == 0), stop=(kc == 31))
            return ins
        p.emit("tensor", _sums, waits=[("scA", p.c("scA"))], inc="tc")
        p.emit("vector", lambda v: v.tensor_copy(rowsf[0:1, 0:256], psum1[0:1, 0:256]),
               waits=[("tc", p.c("tc"))], inc="vpe")
        p.emit("vector", lambda v: v.reciprocal(rowsf[0:1, 256:512], rowsf[0:1, 0:256]),
               waits=[("vpe", p.c("vpe"))], inc="vpe")
        p.emit("tensor", lambda t: t.matmul(psm[:, 0:256], onesf[0:1, 0:128],
                                            rowsf[0:1, 256:512], start=True, stop=True),
               waits=[("vpe", p.c("vpe"))], inc="tc")
        p.emit("vector", lambda v: v.tensor_copy(recipb[:, 5120:5376], psm[:, 0:256]),
               waits=[("tc", p.c("tc"))], inc="vpe")
        for vd in range(8):
            pb = ps[2 + vd % 2]

            def _pv(t, vd=vd, pb=pb):
                ins = None
                for kc in range(32):
                    ins = t.matmul(pb[:, 0:256], v0[:, kc * 1024 + vd * 128: kc * 1024 + vd * 128 + 128],
                                   pt[:, kc * 256:(kc + 1) * 256], start=(kc == 0), stop=(kc == 31))
                return ins
            p.emit("tensor", _pv, waits=[("vpe", p.c("vpe")), ("vpv", p.c("vpv"))], inc="tc")
            p.emit("vector", lambda v, vd=vd, pb=pb: v.tensor_tensor(
                attn0fm[:, 8192 + vd * 256: 8192 + (vd + 1) * 256], pb[:, 0:256],
                recipb[:, 5120:5376], OP.mult),
                waits=[("tc", p.c("tc"))], inc="vpv")
        for ob in range(8):
            p.dma(g, wbuf[:, 0:8 * 512].rearrange("p (c n) -> p c n", c=8),
                  wo[:, ob * 512:(ob + 1) * 512].rearrange("(c p) n -> p c n", p=128),
                  waits=[("tc", p.c("tc"))], inc="ldw")
            p.dma(g, brow2[0:1, :], bo[0:1, ob * 512:(ob + 1) * 512],
                  waits=[("tc", p.c("tc"))], inc="ldw")
            for j2 in range(2):
                j = 2 * gi + j2
                pa = ps[j2 % 2]
                p.dma("sync", stage2[:, :], x1part[j * 128:(j + 1) * 128, ob * 512:(ob + 1) * 512],
                      waits=[("vpe", p.c("vpe"))], inc="ldx")

                def _mm(t, j2=j2, pa=pa):
                    t.matmul(pa[:, 0:512], ones1[0:1, 0:128], brow2[0:1, :], start=True, stop=False)
                    ins = None
                    for c in range(8):
                        ins = t.matmul(pa[:, 0:512],
                                       attn0fm[:, 8192 + c * 256 + j2 * 128: 8192 + c * 256 + j2 * 128 + 128],
                                       wbuf[:, c * 512:(c + 1) * 512], start=False, stop=(c == 7))
                    return ins
                p.emit("tensor", _mm, waits=[("ldw", p.c("ldw")), ("vpv", p.c("vpv")),
                                             ("vpe", p.c("vpe"))], inc="tc")
                p.emit("vector", lambda v, pa=pa: v.scalar_tensor_tensor(
                    stage[:, :], pa[:, 0:512], 1.0, stage2[:, :], OP.mult, OP.add),
                    waits=[("tc", p.c("tc")), ("ldx", p.c("ldx")), ("stx", p.c("stx"))], inc="vpe")
                p.dma("sync", x1part[j * 128:(j + 1) * 128, ob * 512:(ob + 1) * 512], stage[:, :],
                      waits=[("vpe", p.c("vpe"))], inc="stx")
    L = dict(locals())
    return _finish_launch2(nc, es, p, L)


def _finish_launch2(nc, es, p, L):
    g = "gpsimd"
    arena = L["arena"]; wbuf = L["wbuf"]; r3 = L["r3"]; r4 = L["r4"]; r5 = L["r5"]
    r3f = L["r3f"]; r4f = L["r4f"]
    stage = L["stage"]; stage2 = L["stage2"]; stageb = L["stageb"]; stats = L["stats"]
    brow2 = L["brow2"]; ones1 = L["ones1"]; onesf = L["onesf"]; onescol = L["onescol"]
    identb = L["identb"]; cossb = L["cossb"]; sinsb = L["sinsb"]
    wqb = L["wqb"]; wkvb = L["wkvb"]; bthrb = L["bthrb"]; combo = L["combo"]
    gatew = L["gatew"]; hgfm = L["hgfm"]; bcatb = L["bcatb"]
    hgbuf = L["hgbuf"]; rtmp = L["rtmp"]; rtmp2 = L["rtmp2"]
    ps = L["ps"]; pst1 = L["pst1"]; pst2 = L["pst2"]; psum1 = L["psum1"]; psm = L["psm"]
    x1part = L["x1part"]; x0 = L["x0"]
    xn1fm = arena
    qa = L["qa"]; qbn = L["qbn"]; qbp = L["qbp"]; kva = L["kva"]
    kvbk = L["kvbk"]; kvbv = L["kvbv"]; combow = L["combow"]
    wbaset = L["wbaset"]; bbase = L["bbase"]; acat = L["acat"]; bcat_d = L["bcat_d"]
    q1nfm_d = L["q1nfm_d"]; q1pe_d = L["q1pe_d"]; k1nfm_d = L["k1nfm_d"]
    k1pe_d = L["k1pe_d"]; v1tm_d = L["v1tm_d"]; ikfm_d = L["ikfm_d"]
    iqfm_d = L["iqfm_d"]; iw_d = L["iw_d"]; x2p_d = L["x2p_d"]
    idx_wq = None

    x1tile = r3f
    xn1tile = r3
    junk = r3
    for j in range(NT):
        p.dma(g, x1tile[:, 0:D], x1part[j * 128:(j + 1) * 128, :],
              waits=[("sc", p.c("sc")), ("stx", p.c("stx")), ("tvx", p.c("tvx"))], inc="ldx2")
        p.emit("vector", lambda v: v.memset(stats[:, 15:16], 0.0),
               waits=[("ldx2", p.c("ldx2"))], inc="vc")
        emit_ln(p, x1tile[:, 0:D], xn1tile[:, 8192:8192 + D], stats, junk[:, 12288:12288 + D])
        emit_transpose_tile(
            p, lambda c: xn1tile[:, 8192 + c * 128: 8192 + (c + 1) * 128],
            lambda c, j=j: xn1fm[:, c * TOK + j * 128: c * TOK + j * 128 + 128],
            identb[:, :], (pst1, pst2), NCD,
            extra_waits=[("sc", p.c("sc")), ("tc", p.c("tc"))])

    def tm_proj(wsrc, col0, ncols, evac_fn, nchunks=NCD, lhs_fm=None, lhs_tok=TOK,
                extra_w=()):
        nq = 4 if ncols < 256 else 1
        cq = nchunks // nq
        for q_ in range(nq):
            p.dma(g, wbuf[:, q_ * cq * ncols:(q_ + 1) * cq * ncols].rearrange(
                "p (c n) -> p c n", c=cq),
                wsrc[q_ * cq * 128:(q_ + 1) * cq * 128, col0:col0 + ncols].rearrange(
                    "(c p) n -> p c n", p=128),
                waits=[("tc", p.c("tc"))], inc="ldw")
        src = lhs_fm if lhs_fm is not None else xn1fm
        for j in range(NT):
            pa = ps[j % 2]

            def _mm(t, j=j, pa=pa):
                ins = None
                for c in range(nchunks):
                    ins = t.matmul(pa[:, 0:ncols],
                                   src[:, c * lhs_tok + j * 128: c * lhs_tok + j * 128 + 128],
                                   wbuf[:, c * ncols:(c + 1) * ncols],
                                   start=(c == 0), stop=(c == nchunks - 1))
                return ins
            p.emit("tensor", _mm, waits=[("ldw", p.c("ldw")), ("tvx", p.c("tvx")),
                                         ("vpe", p.c("vpe"))] + list(extra_w), inc="tc")
            evac_fn(j, pa)

    def combo_evac(j, pa):
        p.emit("vector", lambda v, j=j, pa=pa: v.tensor_copy(
            combo[:, j * 32: j * 32 + 32], pa[:, 0:32]), waits=[("tc", p.c("tc"))], inc="vpe")
    tm_proj(combow, 0, 32, combo_evac)
    for j in range(NT):
        p.emit("vector", lambda v, j=j: v.tensor_scalar(
            stage[:, 0:16], combo[:, j * 32 + 9: j * 32 + 25], 0.25, None, OP.mult),
            waits=[("vpe", p.c("vpe")), ("stx", p.c("stx"))], inc="vpe")
        p.dma("sync", iw_d[j * 128:(j + 1) * 128, :], stage[:, 0:16],
              waits=[("vpe", p.c("vpe"))], inc="stx")

    qlat = r4
    for blk in range(3):
        def qlat_evac(j, pa, blk=blk):
            p.emit("vector", lambda v, j=j, pa=pa, blk=blk: v.tensor_copy(
                qlat[:, j * 1536 + blk * 512: j * 1536 + (blk + 1) * 512], pa[:, 0:512]),
                waits=[("tc", p.c("tc"))], inc="vpe")
        tm_proj(qa, blk * 512, 512, qlat_evac)
    def emit_rms_inplace(buf_ap, dim, wb, jtag):
        p.emit("scalar", lambda s: s.activation(
            junk[:, 12288:12288 + dim], buf_ap, AF.Square, accum_out=stats[:, 0:1]),
            waits=[("vpe", p.c("vpe")), ("vc", p.c("vc"))], inc="sc")
        p.emit("vector", lambda v: v.tensor_scalar(
            stats[:, 1:2], stats[:, 0:1], 1.0 / dim, 1e-5, OP.mult, OP.add),
            waits=[("sc", p.c("sc"))], inc="vc")
        p.emit("scalar", lambda s: s.activation(stats[:, 2:3], stats[:, 1:2], AF.Ln),
               waits=[("vc", p.c("vc"))], inc="sc")
        p.emit("scalar", lambda s: s.activation(stats[:, 1:2], stats[:, 2:3], AF.Exp,
                                                scale=-0.5), inc="sc")
        p.emit("vector", lambda v: v.tensor_scalar(buf_ap, buf_ap, stats[:, 1:2], None,
                                                   OP.mult),
               waits=[("sc", p.c("sc"))], inc="vc")
        p.emit("vector", lambda v: v.tensor_tensor(buf_ap, buf_ap, wb, OP.mult),
               waits=[("vc", p.c("vc"))], inc="vpe")
    for j in range(NT):
        emit_rms_inplace(qlat[:, j * 1536:(j + 1) * 1536], 1536, wqb[:, :], j)
    if L["dbg_qlat"] is not None:
        p.dma("sync", L["dbg_qlat"][:, :], qlat[:, 0:8 * 1536], waits=[("vpe", p.c("vpe"))], inc="st")
        p.dma("sync", L["dbg_combo"][:, :], combo[:, :], waits=[("vpe", p.c("vpe"))], inc="st")

    latfm = r5
    for st in range(2):
        base_tvx = p.c("tvx")
        for jj in range(4):
            j = st * 4 + jj
            emit_transpose_tile(
                p, lambda c, j=j: qlat[:, j * 1536 + c * 128: j * 1536 + (c + 1) * 128],
                lambda c, jj=jj: latfm[:, c * 512 + jj * 128: c * 512 + jj * 128 + 128],
                identb[:, :], (pst1, pst2), 12,
                extra_waits=[("vpe", p.c("vpe")), ("tc", p.c("tc"))])
        if L["dbg_latfm"] is not None and st == 0:
            p.dma("sync", L["dbg_latfm"][:, :], latfm[:, 0:6144],
                  waits=[("tvx", p.c("tvx"))], inc="st")
        for wblk in range(8):
            p.dma(g, wbuf[:, 0:12 * 512].rearrange("p (c n) -> p c n", c=12),
                  qbn[:, wblk * 512:(wblk + 1) * 512].rearrange("(c p) n -> p c n", p=128),
                  waits=[("tc", p.c("tc"))], inc="ldw")
            for fsub in range(4):
                pa = ps[fsub % 2]

                def _mm(t, fsub=fsub, pa=pa):
                    ins = None
                    for c in range(12):
                        ins = t.matmul(pa[:, 0:512],
                                       wbuf[:, c * 512 + fsub * 128: c * 512 + fsub * 128 + 128],
                                       latfm[:, c * 512:(c + 1) * 512],
                                       start=(c == 0), stop=(c == 11))
                    return ins
                p.emit("tensor", _mm, waits=[("ldw", p.c("ldw")), ("tvx", p.c("tvx")),
                                             ("vpe", p.c("vpe"))], inc="tc")
                p.emit("vector", lambda v, pa=pa: v.tensor_copy(stageb[:, :], pa[:, 0:512]),
                       waits=[("tc", p.c("tc")), ("stx", p.c("stx"))], inc="vpe")
                p.dma("sync", q1nfm_d[wblk * 512 + fsub * 128: wblk * 512 + (fsub + 1) * 128,
                                      st * 512:(st + 1) * 512], stageb[:, :],
                      waits=[("vpe", p.c("vpe"))], inc="stx")
        for wblk in range(4):
            p.dma(g, wbuf[:, 0:12 * 512].rearrange("p (c n) -> p c n", c=12),
                  qbp[:, wblk * 512:(wblk + 1) * 512].rearrange("(c p) n -> p c n", p=128),
                  waits=[("tc", p.c("tc"))], inc="ldw")
            for jj in range(4):
                j = st * 4 + jj
                pa = ps[jj % 2]

                def _mm(t, jj=jj, pa=pa):
                    ins = None
                    for c in range(12):
                        ins = t.matmul(pa[:, 0:512],
                                       latfm[:, c * 512 + jj * 128: c * 512 + jj * 128 + 128],
                                       wbuf[:, c * 512:(c + 1) * 512],
                                       start=(c == 0), stop=(c == 11))
                    return ins
                p.emit("tensor", _mm, waits=[("ldw", p.c("ldw")), ("tvx", p.c("tvx")),
                                             ("vpe", p.c("vpe"))], inc="tc")
                emit_rope_block(p, pa, j, 8, cossb, sinsb, rtmp, rtmp2, stageb)
                p.dma("sync", q1pe_d[j * 128:(j + 1) * 128, wblk * 512:(wblk + 1) * 512],
                      stageb[:, :], waits=[("vpe", p.c("vpe"))], inc="stx")
    return _finish_launch2b(nc, es, p, L)


def _finish_launch2b(nc, es, p, L):
    g = "gpsimd"
    arena = L["arena"]; wbuf = L["wbuf"]; r3 = L["r3"]; r4 = L["r4"]; r5 = L["r5"]
    r3f = L["r3f"]
    stage = L["stage"]; stage2 = L["stage2"]; stageb = L["stageb"]; stats = L["stats"]
    brow2 = L["brow2"]; ones1 = L["ones1"]; onescol = L["onescol"]
    identb = L["identb"]; cossb = L["cossb"]; sinsb = L["sinsb"]
    wkvb = L["wkvb"]; bthrb = L["bthrb"]; combo = L["combo"]
    gatew = L["gatew"]; hgfm = L["hgfm"]; bcatb = L["bcatb"]
    hgbuf = L["hgbuf"]; rtmp = L["rtmp"]; rtmp2 = L["rtmp2"]
    ps = L["ps"]; pst1 = L["pst1"]; pst2 = L["pst2"]; psm = L["psm"]
    x1part = L["x1part"]
    kva = L["kva"]; kvbk = L["kvbk"]; kvbv = L["kvbv"]
    wbaset = L["wbaset"]; bbase = L["bbase"]; acat = L["acat"]; bcat_d = L["bcat_d"]
    k1nfm_d = L["k1nfm_d"]; k1pe_d = L["k1pe_d"]; v1tm_d = L["v1tm_d"]
    ikfm_d = L["ikfm_d"]; iqfm_d = L["iqfm_d"]; x2p_d = L["x2p_d"]
    idx_wq_d = L["idx_wq_d"]; idx_wk_d = L["idx_wk_d"]
    xn1fm = arena
    junk = r3

    ckv = r5
    latkvfm = r5
    for blk, (c0, nc_) in enumerate([(0, 512), (512, 64)]):
        p.dma(g, wbuf[:, 0:NCD * nc_].rearrange("p (c n) -> p c n", c=NCD),
              kva[:, c0:c0 + nc_].rearrange("(c p) n -> p c n", p=128),
              waits=[("tc", p.c("tc"))], inc="ldw")
        for j in range(NT):
            pa = ps[j % 2]

            def _mm(t, j=j, pa=pa, nc_=nc_):
                ins = None
                for c in range(NCD):
                    ins = t.matmul(pa[:, 0:nc_],
                                   xn1fm[:, c * TOK + j * 128: c * TOK + j * 128 + 128],
                                   wbuf[:, c * nc_:(c + 1) * nc_],
                                   start=(c == 0), stop=(c == NCD - 1))
                return ins
            p.emit("tensor", _mm, waits=[("ldw", p.c("ldw")), ("vpe", p.c("vpe"))], inc="tc")
            p.emit("vector", lambda v, j=j, pa=pa, c0=c0, nc_=nc_: v.tensor_copy(
                ckv[:, j * 576 + c0: j * 576 + c0 + nc_], pa[:, 0:nc_]),
                waits=[("tc", p.c("tc"))], inc="vpe")
    for j in range(NT):
        buf = ckv[:, j * 576: j * 576 + 512]
        p.emit("scalar", lambda s, buf=buf: s.activation(
            junk[:, 0:512], buf, AF.Square, accum_out=stats[:, 0:1]),
            waits=[("vpe", p.c("vpe")), ("vc", p.c("vc"))], inc="sc")
        p.emit("vector", lambda v: v.tensor_scalar(
            stats[:, 1:2], stats[:, 0:1], 1.0 / 512, 1e-5, OP.mult, OP.add),
            waits=[("sc", p.c("sc"))], inc="vc")
        p.emit("scalar", lambda s: s.activation(stats[:, 2:3], stats[:, 1:2], AF.Ln),
               waits=[("vc", p.c("vc"))], inc="sc")
        p.emit("scalar", lambda s: s.activation(stats[:, 1:2], stats[:, 2:3], AF.Exp,
                                                scale=-0.5), inc="sc")
        p.emit("vector", lambda v, buf=buf: v.tensor_scalar(buf, buf, stats[:, 1:2], None,
                                                            OP.mult),
               waits=[("sc", p.c("sc"))], inc="vc")
        p.emit("vector", lambda v, buf=buf: v.tensor_tensor(buf, buf, wkvb[:, :], OP.mult),
               waits=[("vc", p.c("vc"))], inc="vpe")
        pe = ckv[:, j * 576 + 512: j * 576 + 576]
        cosv = cossb[:, j * 32:(j + 1) * 32].unsqueeze(1).to_broadcast([128, 1, 32])
        sinv = sinsb[:, j * 32:(j + 1) * 32].unsqueeze(1).to_broadcast([128, 1, 32])
        src = pe.rearrange("p (h r two) -> p h r two", h=1, two=2)
        dst = stageb[:, 0:64].rearrange("p (h r two) -> p h r two", h=1, two=2)
        t1 = rtmp[:, 0:32].rearrange("p (h r) -> p h r", h=1)
        t2 = rtmp2[:, 0:32].rearrange("p (h r) -> p h r", h=1)

        def _r(v, src=src, dst=dst, t1=t1, t2=t2, cosv=cosv, sinv=sinv):
            v.tensor_tensor(t1, src[:, :, :, 0], cosv, OP.mult)
            v.tensor_tensor(t2, src[:, :, :, 1], sinv, OP.mult)
            v.tensor_tensor(dst[:, :, :, 0], t1, t2, OP.subtract)
            v.tensor_tensor(t1, src[:, :, :, 1], cosv, OP.mult)
            v.tensor_tensor(t2, src[:, :, :, 0], sinv, OP.mult)
            return v.tensor_tensor(dst[:, :, :, 1], t1, t2, OP.add)
        p.emit("vector", _r, waits=[("stx", p.c("stx"))], inc="vpe")
        p.dma("sync", k1pe_d[j * 128:(j + 1) * 128, :], stageb[:, 0:64],
              waits=[("vpe", p.c("vpe"))], inc="stx")
    for st in range(2):
        for jj in range(4):
            j = st * 4 + jj
            emit_transpose_tile(
                p, lambda c, j=j: ckv[:, j * 576 + c * 128: j * 576 + (c + 1) * 128],
                lambda c, jj=jj: latkvfm[:, 4608 + c * 512 + jj * 128: 4608 + c * 512 + jj * 128 + 128],
                identb[:, :], (pst1, pst2), 4,
                extra_waits=[("vpe", p.c("vpe")), ("tc", p.c("tc"))])
        for wblk in range(8):
            p.dma(g, wbuf[:, 0:4 * 512].rearrange("p (c n) -> p c n", c=4),
                  kvbk[:, wblk * 512:(wblk + 1) * 512].rearrange("(c p) n -> p c n", p=128),
                  waits=[("tc", p.c("tc"))], inc="ldw")
            for fsub in range(4):
                pa = ps[fsub % 2]

                def _mm(t, fsub=fsub, pa=pa):
                    ins = None
                    for c in range(4):
                        ins = t.matmul(pa[:, 0:512],
                                       wbuf[:, c * 512 + fsub * 128: c * 512 + fsub * 128 + 128],
                                       latkvfm[:, 4608 + c * 512: 4608 + (c + 1) * 512],
                                       start=(c == 0), stop=(c == 3))
                    return ins
                p.emit("tensor", _mm, waits=[("ldw", p.c("ldw")), ("tvx", p.c("tvx")),
                                             ("vpe", p.c("vpe"))], inc="tc")
                p.emit("vector", lambda v, pa=pa: v.tensor_copy(stageb[:, :], pa[:, 0:512]),
                       waits=[("tc", p.c("tc")), ("stx", p.c("stx"))], inc="vpe")
                p.dma("sync", k1nfm_d[wblk * 512 + fsub * 128: wblk * 512 + (fsub + 1) * 128,
                                      st * 512:(st + 1) * 512], stageb[:, :],
                      waits=[("vpe", p.c("vpe"))], inc="stx")
        for wblk in range(8):
            p.dma(g, wbuf[:, 0:4 * 512].rearrange("p (c n) -> p c n", c=4),
                  kvbv[:, wblk * 512:(wblk + 1) * 512].rearrange("(c p) n -> p c n", p=128),
                  waits=[("tc", p.c("tc"))], inc="ldw")
            for jj in range(4):
                j = st * 4 + jj
                pa = ps[jj % 2]

                def _mm(t, jj=jj, pa=pa):
                    ins = None
                    for c in range(4):
                        ins = t.matmul(pa[:, 0:512],
                                       latkvfm[:, 4608 + c * 512 + jj * 128: 4608 + c * 512 + jj * 128 + 128],
                                       wbuf[:, c * 512:(c + 1) * 512],
                                       start=(c == 0), stop=(c == 3))
                    return ins
                p.emit("tensor", _mm, waits=[("ldw", p.c("ldw")), ("tvx", p.c("tvx")),
                                             ("vpe", p.c("vpe"))], inc="tc")
                p.emit("vector", lambda v, pa=pa: v.tensor_copy(stageb[:, :], pa[:, 0:512]),
                       waits=[("tc", p.c("tc")), ("stx", p.c("stx"))], inc="vpe")
                p.dma("sync", v1tm_d[j * 128:(j + 1) * 128, wblk * 512:(wblk + 1) * 512],
                      stageb[:, :], waits=[("vpe", p.c("vpe"))], inc="stx")

    for wblk in range(4):
        p.dma(g, wbuf[:, 0:NCD * 512].rearrange("p (c n) -> p c n", c=NCD),
              idx_wq_d[:, wblk * 512:(wblk + 1) * 512].rearrange("(c p) n -> p c n", p=128),
              waits=[("tc", p.c("tc"))], inc="ldw")
        for fsub in range(4):
            for st in range(2):
                pa = ps[(fsub * 2 + st) % 2]

                def _mm(t, fsub=fsub, st=st, pa=pa):
                    ins = None
                    for c in range(NCD):
                        ins = t.matmul(pa[:, 0:512],
                                       wbuf[:, c * 512 + fsub * 128: c * 512 + fsub * 128 + 128],
                                       xn1fm[:, c * TOK + st * 512: c * TOK + st * 512 + 512],
                                       start=(c == 0), stop=(c == NCD - 1))
                    return ins
                p.emit("tensor", _mm, waits=[("ldw", p.c("ldw")), ("vpe", p.c("vpe"))], inc="tc")
                p.emit("vector", lambda v, pa=pa: v.tensor_copy(stageb[:, :], pa[:, 0:512]),
                       waits=[("tc", p.c("tc")), ("stx", p.c("stx"))], inc="vpe")
                p.dma("sync", iqfm_d[wblk * 512 + fsub * 128: wblk * 512 + (fsub + 1) * 128,
                                     st * 512:(st + 1) * 512], stageb[:, :],
                      waits=[("vpe", p.c("vpe"))], inc="stx")
    p.dma(g, wbuf[:, 0:NCD * 128].rearrange("p (c n) -> p c n", c=NCD),
          idx_wk_d.rearrange("(c p) n -> p c n", p=128),
          waits=[("tc", p.c("tc"))], inc="ldw")
    for st in range(2):
        pa = ps[st % 2]

        def _mm(t, st=st, pa=pa):
            ins = None
            for c in range(NCD):
                ins = t.matmul(pa[:, 0:512], wbuf[:, c * 128:(c + 1) * 128],
                               xn1fm[:, c * TOK + st * 512: c * TOK + st * 512 + 512],
                               start=(c == 0), stop=(c == NCD - 1))
            return ins
        p.emit("tensor", _mm, waits=[("ldw", p.c("ldw")), ("vpe", p.c("vpe"))], inc="tc")
        p.emit("vector", lambda v, pa=pa: v.tensor_copy(stageb[:, :], pa[:, 0:512]),
               waits=[("tc", p.c("tc")), ("stx", p.c("stx"))], inc="vpe")
        p.dma("sync", ikfm_d[:, st * 512:(st + 1) * 512], stageb[:, :],
              waits=[("vpe", p.c("vpe"))], inc="stx")

    acatb = wbuf
    p.dma(g, acatb[:, 0:NCD * 128].rearrange("p (c n) -> p c n", c=NCD),
          acat.rearrange("(c p) n -> p c n", p=128), waits=[("tc", p.c("tc"))], inc="ldw")
    p.dma(g, bcatb[:, :], bcat_d, waits=[("tc", p.c("tc"))], inc="ldw")
    for j in range(NT):
        cg = combo[:, j * 32: j * 32 + 8]
        p.emit("vector", lambda v, cg=cg: v.reduce_max(stats[:, 0:1], cg, AX, negate=True),
               waits=[("vpe", p.c("vpe"))], inc="vc")
        p.emit("scalar", lambda s, cg=cg: s.activation(gatew[:, 0:8], cg, AF.Exp,
                                                       bias=stats[:, 0:1], scale=1.0,
                                                       accum_out=stats[:, 1:2]),
               waits=[("vc", p.c("vc"))], inc="sc")
        p.emit("vector", lambda v: v.reciprocal(stats[:, 2:3], stats[:, 1:2]),
               waits=[("sc", p.c("sc"))], inc="vc")
        p.emit("vector", lambda v: v.tensor_scalar(gatew[:, 0:8], gatew[:, 0:8],
                                                   stats[:, 2:3], None, OP.mult),
               waits=[("vc", p.c("vc"))], inc="vc")
        p.emit("vector", lambda v, j=j: v.tensor_tensor(
            stats[:, 3:4], combo[:, j * 32 + 8: j * 32 + 9], bthrb[:, :], OP.add),
               waits=[("vc", p.c("vc"))], inc="vc")
        p.emit("scalar", lambda s: s.activation(stats[:, 4:5], stats[:, 3:4], AF.Exp,
                                                scale=-1.0),
               waits=[("vc", p.c("vc"))], inc="sc")
        p.emit("vector", lambda v: v.tensor_scalar(stats[:, 4:5], stats[:, 4:5], 1.0,
                                                   None, OP.add),
               waits=[("sc", p.c("sc")), ("vc", p.c("vc"))], inc="vc")
        p.emit("vector", lambda v: v.reciprocal(stats[:, 5:6], stats[:, 4:5]),
               waits=[("vc", p.c("vc"))], inc="vc")
        p.emit("vector", lambda v: v.tensor_scalar(stats[:, 6:7], stats[:, 5:6], 0.125,
                                                   None, OP.mult),
               waits=[("vc", p.c("vc"))], inc="vc")
        p.emit("vector", lambda v: v.tensor_scalar(gatew[:, 8:16], gatew[:, 0:8],
                                                   stats[:, 6:7], 0.0, OP.subtract, OP.max),
               waits=[("vc", p.c("vc"))], inc="vc")
        p.emit("vector", lambda v: v.reduce_sum(stats[:, 7:8], gatew[:, 8:16], AX),
               waits=[("vc", p.c("vc"))], inc="vc")
        p.emit("vector", lambda v: v.tensor_scalar(stats[:, 8:9], stats[:, 7:8], 0.0, None,
                                                   OP.is_equal),
               waits=[("vc", p.c("vc"))], inc="vc")
        p.emit("vector", lambda v: v.tensor_tensor(stats[:, 7:8], stats[:, 7:8],
                                                   stats[:, 8:9], OP.add),
               waits=[("vc", p.c("vc"))], inc="vc")
        p.emit("vector", lambda v: v.reciprocal(stats[:, 9:10], stats[:, 7:8]),
               waits=[("vc", p.c("vc"))], inc="vc")
        p.emit("vector", lambda v: v.tensor_scalar(gatew[:, 8:16], gatew[:, 8:16],
                                                   stats[:, 9:10], 2.0, OP.mult, OP.mult),
               waits=[("vc", p.c("vc"))], inc="vc")
        pa = ps[j % 2]

        def _mm(t, j=j, pa=pa):
            ins = None
            for c in range(NCD):
                ins = t.matmul(pa[:, 0:128], xn1fm[:, c * TOK + j * 128: c * TOK + j * 128 + 128],
                               acatb[:, c * 128:(c + 1) * 128], start=(c == 0), stop=(c == NCD - 1))
            return ins
        p.emit("tensor", _mm, waits=[("ldw", p.c("ldw")), ("vpe", p.c("vpe"))], inc="tc")
        p.emit("vector", lambda v, pa=pa: v.tensor_copy(stageb[:, 0:128], pa[:, 0:128]),
               waits=[("tc", p.c("tc")), ("vc", p.c("vc"))], inc="vc")

        def _hg(v):
            ins = None
            for e in range(8):
                ins = v.tensor_scalar(hgbuf[:, e * 16:(e + 1) * 16], stageb[:, e * 16:(e + 1) * 16],
                                      gatew[:, 8 + e: 9 + e], None, OP.mult)
            return ins
        if L["dbg_wn"] is not None:
            p.dma("sync", L["dbg_st"][j], stats[:, :], waits=[("vc", p.c("vc"))], inc="st")
            p.emit("vector", lambda v, j=j: v.tensor_copy(
                L["dbg_wn"].tensor.ap()[:, j * 16:(j + 1) * 16], gatew[:, 0:16]) if False else v.tensor_copy(stage[:, 0:16], gatew[:, 0:16]), inc="vc")
            p.dma("sync", L["dbg_wn"][:, j * 16:(j + 1) * 16], stage[:, 0:16],
                  waits=[("vc", p.c("vc"))], inc="st")
            p.dma("sync", L["dbg_h"][:, j * 128:(j + 1) * 128], stageb[:, 0:128],
                  waits=[("vc", p.c("vc"))], inc="st")
        p.emit("vector", _hg, waits=[("txp", p.c("txp")), ("vc", p.c("vc"))], inc="vc")
        emit_transpose_tile(p, lambda c: hgbuf[:, 0:128],
                            lambda c, j=j: hgfm[:, j * 128:(j + 1) * 128],
                            identb[:, :], (pst1, pst2), 1,
                            extra_waits=[("vc", p.c("vc"))])
    if L["dbg_hgfm"] is not None:
        p.dma("sync", L["dbg_hgfm"][:, :], hgfm[:, :], waits=[("tvx", p.c("tvx"))], inc="st")
    for ob in range(8):
        p.dma(g, wbuf[:, :].rearrange("p (c n) -> p c n", c=NCD),
              wbaset[:, ob * 512:(ob + 1) * 512].rearrange("(c p) n -> p c n", p=128),
              waits=[("tc", p.c("tc"))], inc="ldw")
        p.dma(g, brow2[0:1, :], bbase[0:1, ob * 512:(ob + 1) * 512],
              waits=[("tc", p.c("tc"))], inc="ldw")
        for j in range(NT):
            pa = ps[j % 2]
            p.dma("sync", stage2[:, :], x1part[j * 128:(j + 1) * 128, ob * 512:(ob + 1) * 512],
                  waits=[("vpe", p.c("vpe"))], inc="ldx")

            def _mm(t, j=j, pa=pa, ob=ob):
                t.matmul(pa[:, 0:512], ones1[0:1, 0:128], brow2[0:1, :], start=True, stop=False)
                for c in range(NCD):
                    t.matmul(pa[:, 0:512], xn1fm[:, c * TOK + j * 128: c * TOK + j * 128 + 128],
                             wbuf[:, c * 512:(c + 1) * 512], start=False, stop=False)
                return t.matmul(pa[:, 0:512], hgfm[:, j * 128:(j + 1) * 128],
                                bcatb[:, ob * 512:(ob + 1) * 512], start=False, stop=True)
            p.emit("tensor", _mm, waits=[("ldw", p.c("ldw")), ("tvx", p.c("tvx")),
                                         ("vpe", p.c("vpe"))], inc="tc")
            p.emit("vector", lambda v, pa=pa: v.scalar_tensor_tensor(
                stage[:, :], pa[:, 0:512], 1.0, stage2[:, :], OP.mult, OP.add),
                waits=[("tc", p.c("tc")), ("ldx", p.c("ldx")), ("stx", p.c("stx"))], inc="vpe")
            p.dma("sync", x2p_d[j * 128:(j + 1) * 128, ob * 512:(ob + 1) * 512], stage[:, :],
                  waits=[("vpe", p.c("vpe"))], inc="stx")

    X = L["X"]
    q1pe_d = L["q1pe_d"]; k1pe_d = L["k1pe_d"]
    q1pT_d = X["q1pT_d"]; kpeT_d = X["kpeT_d"]
    for j in range(NT):
        p.dma(g, wbuf[:, 0:2048], q1pe_d[j * 128:(j + 1) * 128, :],
              waits=[("stx", p.c("stx")), ("txq", p.c("txq"))], inc="ldq")
        for c in range(16):
            bank = (pst1, pst2)[c % 2]
            half = (c % 2) * 128
            stq0 = p.c("stq")
            p.emit("tensor", lambda t, c=c, bank=bank: t.transpose(
                bank[0:128, 0:128], wbuf[:, c * 128:(c + 1) * 128], identb[:, :]),
                waits=[("ldq", p.c("ldq")), ("txp", p.c("txp")),
                       ("tvq", max(0, p.c("tvq") - 1)), ("tvx", p.c("tvx"))], inc="txq")
            p.emit("vector", lambda v, half=half, bank=bank: v.tensor_copy(
                stageb[:, half:half + 128], bank[0:128, 0:128]),
                waits=[("txq", p.c("txq")), ("stq", max(0, stq0 - 16)),
                       ("stx", p.c("stx"))], inc="tvq")
            p.dma("sync", q1pT_d[c * 128:(c + 1) * 128, j * 128:(j + 1) * 128],
                  stageb[:, half:half + 128], waits=[("tvq", p.c("tvq"))], inc="stq")
    for j in range(NT):
        p.dma(g, wbuf[:, 2048:2048 + 64], k1pe_d[j * 128:(j + 1) * 128, :],
              waits=[("txq", p.c("txq"))], inc="ldq")
        bank = (pst1, pst2)[j % 2]
        half = (j % 2) * 128
        stq0 = p.c("stq")
        p.emit("tensor", lambda t, bank=bank: t.transpose(
            bank[0:64, 0:128], wbuf[:, 2048:2048 + 64], identb[:, :]),
            waits=[("ldq", p.c("ldq")), ("tvq", max(0, p.c("tvq") - 1))], inc="txq")
        p.emit("vector", lambda v, half=half, bank=bank: v.tensor_copy(
            stageb[0:64, half:half + 128], bank[0:64, 0:128]),
            waits=[("txq", p.c("txq")), ("stq", max(0, stq0 - 16))], inc="tvq")
        p.dma("sync", kpeT_d[:, j * 128:(j + 1) * 128], stageb[0:64, half:half + 128],
              waits=[("tvq", p.c("tvq"))], inc="stq")


SC1 = float(192.0 ** -0.5)
NBIS = 24


def emit_phase3(nc, es, p, A, X):
    iw_in = X["iw_d"]; cmasks = X["cmasks"]; x2part = X["x2p_d"]
    ow_d = X["ow"]; identb_d = X["identb_d"]; out_d = X["out_d"]
    q1nfm_d = X["q1nfm_d"]; q1pT_d = X["q1pT_d"]; iqfm_d = X["iqfm_d"]
    ex_k1 = X["ex_k1"]; ex_v1 = X["ex_v1"]; ex_kpe = X["ex_kpe"]; ex_ik = X["ex_ik"]
    dbg_bis = dbg_keep = dbg_sums = dbg_attn = None

    q1n = A.sb("q1n", [128, 32 * 256], BF16)
    q1p = A.sb("q1p", [64, 32 * 256], BF16)
    iqg = A.sb("iqg", [128, 16 * 256], BF16)
    iwg = A.sb("iwg", [128, 2 * 16], F32)
    ik_sb = A.sb("ik_sb", [128, 4096], BF16)
    kpe_sb = A.sb("kpe_sb", [64, 4096], BF16)
    acc = A.sb("acc", [128, 4096], F32)
    cmask_sb = A.sb("cmask_sb", [128, 4096], BF16)
    keep = A.sb("keep", [128, 2 * 4096], BF16)
    keepT = A.sb("keepT", [128, 32 * 256], BF16)
    pt = A.sb("pt_b", [128, 32 * 256], BF16)
    kb_sb = A.sb("kb_sb", [128, 4096], BF16)
    vb_sb = A.sb("vb_sb", [128, 4096], BF16)
    attnfm = A.sb("attnfm", [128, 32 * 256], BF16)
    relub = A.sb("relub", [128, 512], BF16)
    stage = A.sb("stage", [128, 512], F32)
    stage2 = A.sb("stage2", [128, 512], F32)
    stats = A.sb("stats", [128, 16], F32)
    bis = A.sb("bis", [128, 8], F32)
    maski = es.enter_context(nc.sbuf_tensor("maski", [128, 1], mybir.dt.int32))
    rowsf = A.sb("rowsf", [1, 512], F32)
    onesf = A.sb("onesf", [1, 128], F32)
    onescol = A.sb("onescol", [128, 1], BF16)
    ones1 = A.sb("ones1", [1, 512], BF16)
    identb = A.sb("identb_sb", [128, 128], BF16)
    recipb = A.sb("recipb", [128, 256], F32)
    mexpb = A.sb("mexpb3", [128, 1], F32)
    negt = A.sb("negt", [128, 4096], F32)
    wbuf = A.sb("wbuf", [128, 8192], BF16)

    ps = X["ps"]; pst1 = X["pst1"]; pst2 = X["pst2"]
    psum1 = X["psum1"]; psm = X["psm"]
    g = "gpsimd"

    p.dma(g, identb[:, :], identb_d)
    for m_ in range(4):
        for j_ in range(4):
            g0 = 256 * (m_ + 4 * j_)
            p.dma(g, ik_sb[:, g0:g0 + 256],
                  ex_ik[m_ * 128:(m_ + 1) * 128, 256 * j_:256 * j_ + 256],
                  waits=[("cc", 6)], inc="ld")
            p.dma(g, kpe_sb[:, g0:g0 + 256],
                  ex_kpe[m_ * 64:(m_ + 1) * 64, 256 * j_:256 * j_ + 256],
                  waits=[("cc", 6)], inc="ld")
    p.emit("vector", lambda v: v.memset(ones1[:, :], 1.0), inc="vc")
    p.emit("vector", lambda v: v.memset(onesf[:, :], 1.0), inc="vc")
    p.emit("vector", lambda v: v.memset(onescol[:, :], 1.0), inc="vc")
    p.emit("vector", lambda v: v.memset(acc[:, :], 0.0), inc="vc")
    p.emit("vector", lambda v: v.memset(mexpb[:, :], -MEXP), inc="vc")
    p.emit("vector", lambda v: v.memset(keep[:, :], 0.0), inc="vc")

    for gi in range(NG):
        nch = nchpad_group(gi)
        p.dma(g, q1n[:, :].rearrange("p (h t) -> p h t", h=32),
              q1nfm_d.rearrange("(h p) t -> p h t", p=128)[:, :, gi * 256:(gi + 1) * 256],
              waits=[("tc", p.c("tc"))], inc="ldg")
        p.dma(g, q1p[:, :].rearrange("p (h t) -> p h t", h=32),
              q1pT_d.rearrange("(h p) t -> p h t", p=64)[:, :, gi * 256:(gi + 1) * 256],
              waits=[("tc", p.c("tc"))], inc="ldg")
        p.dma(g, iqg[:, :].rearrange("p (h t) -> p h t", h=16),
              iqfm_d.rearrange("(h p) t -> p h t", p=128)[:, :, gi * 256:(gi + 1) * 256],
              waits=[("tc", p.c("tc"))], inc="ldg")
        p.dma(g, iwg[:, :].rearrange("p (j n) -> p j n", j=2),
              iw_in.rearrange("(j p) n -> p j n", p=128)[:, 2 * gi:2 * gi + 2, :],
              waits=[("vc", p.c("vc"))], inc="ldg")

        for j2 in range(2):
            j = 2 * gi + j2
            nchj = nchpad_tile(j)
            kext = nchj * 128
            p.dma(g, cmask_sb[:, 0:kext], cmasks[j, :, 0:kext],
                  waits=[("vq", p.c("vq"))], inc="ldm")
            for n in range(16):
                nkb = (kext + 511) // 512
                for kb in range(nkb):
                    w_ = min(512, kext - kb * 512)
                    pa = ps[(n * nkb + kb) % 2]

                    def _mm(t, n=n, kb=kb, w_=w_, pa=pa, j2=j2):
                        return t.matmul(pa[:, 0:w_], iqg[:, n * 256 + j2 * 128: n * 256 + j2 * 128 + 128],
                                        ik_sb[:, kb * 512: kb * 512 + w_], start=True, stop=True)
                    p.emit("tensor", _mm, waits=[("ldg", p.c("ldg")), ("sq", p.c("sq") - 1),
                                                 ("vc", p.c("vc"))], inc="tc")
                    p.emit("scalar", lambda s, w_=w_, pa=pa: s.activation(
                        relub[:, 0:w_], pa[:, 0:w_], AF.Relu),
                        waits=[("tc", p.c("tc")), ("vq", p.c("vq"))], inc="sq")
                    if n == 0:
                        p.emit("vector", lambda v, kb=kb, w_=w_, j2=j2: v.tensor_scalar(
                            acc[:, kb * 512: kb * 512 + w_], relub[:, 0:w_],
                            iwg[:, j2 * 16: j2 * 16 + 1], None, OP.mult),
                            waits=[("sq", p.c("sq")), ("ldg", p.c("ldg"))], inc="vq")
                    else:
                        p.emit("vector", lambda v, n=n, kb=kb, w_=w_, j2=j2: v.scalar_tensor_tensor(
                            acc[:, kb * 512: kb * 512 + w_], relub[:, 0:w_],
                            iwg[:, j2 * 16 + n: j2 * 16 + n + 1],
                            acc[:, kb * 512: kb * 512 + w_], OP.mult, OP.add),
                            waits=[("sq", p.c("sq"))], inc="vq")
            p.emit("vector", lambda v, kext=kext: v.tensor_tensor(
                acc[:, 0:kext], acc[:, 0:kext], cmask_sb[:, 0:kext], OP.add),
                waits=[("ldm", p.c("ldm")), ("vq", p.c("vq"))], inc="vq")
            glob_T_min = 2 * 0 + 8 * (j // 2) + (j % 2)
            if glob_T_min >= 8 or True:
                p.emit("vector", lambda v, kext=kext: v.reduce_max(
                    bis[:, 1:2], acc[:, 0:kext], AX), inc="vq")
                p.emit("vector", lambda v, kext=kext: v.scalar_tensor_tensor(
                    negt[:, 0:kext], acc[:, 0:kext], -1.0, cmask_sb[:, 0:kext],
                    OP.mult, OP.add), waits=[("tc", p.c("tc"))], inc="vq")
                p.emit("vector", lambda v, kext=kext: v.reduce_max(
                    bis[:, 0:1], negt[:, 0:kext], AX), inc="vq")
                p.emit("vector", lambda v: v.tensor_scalar(
                    bis[:, 0:1], bis[:, 0:1], -1.0, None, OP.mult),
                    waits=[("vq", p.c("vq"))], inc="vq")
                for it in range(NBIS):
                    p.emit("vector", lambda v: v.tensor_scalar(
                        bis[:, 2:3], bis[:, 0:1], bis[:, 1:2], 0.5, OP.add, OP.mult),
                        waits=[("vq", p.c("vq"))], inc="vq")
                    p.emit("vector", lambda v, kext=kext: v.tensor_scalar(
                        pt[:, 0:kext], acc[:, 0:kext], bis[:, 2:3], None,
                        OP.is_ge, OP.add, accum_out=bis[:, 3:4]),
                        waits=[("vq", p.c("vq"))], inc="vq")
                    p.emit("vector", lambda v: v.tensor_scalar(
                        maski[:, :], bis[:, 3:4], 1024.0, None, OP.is_ge),
                        waits=[("vq", p.c("vq"))], inc="vq")
                    p.emit("vector", lambda v: v.select(
                        bis[:, 0:1], maski[:, :], bis[:, 2:3], bis[:, 0:1]),
                        waits=[("vq", p.c("vq"))], inc="vq")
                    p.emit("vector", lambda v: v.select(
                        bis[:, 1:2], maski[:, :], bis[:, 1:2], bis[:, 2:3]), inc="vq")
            if dbg_bis is not None:
                p.dma("sync", dbg_bis[j], bis[:, :], waits=[("vq", p.c("vq"))], inc="st")
            p.emit("vector", lambda v, j2=j2, kext=kext: v.tensor_scalar(
                keep[:, j2 * 4096: j2 * 4096 + kext], acc[:, 0:kext], bis[:, 0:1], None,
                OP.is_ge), waits=[("vq", p.c("vq"))], inc="vq")
            if nch * 128 > kext:
                p.emit("vector", lambda v, j2=j2, kext=kext, nch=nch: v.memset(
                    keep[:, j2 * 4096 + kext: j2 * 4096 + nch * 128], 0.0), inc="vq")
        if dbg_keep is not None:
            p.dma("sync", dbg_keep[gi], keep[:, :], waits=[("vq", p.c("vq"))], inc="st")
        for c in range(nch):
            for j2 in range(2):
                base = p.c("tvx")
                bank = (pst1, pst2)[base % 2]
                w = [("vq", p.c("vq"))]
                if c * 2 + j2 >= 2:
                    w.append(("tvx", base - 1))
                p.emit("tensor", lambda t, c=c, j2=j2, bank=bank: t.transpose(
                    bank[:, 0:128], keep[:, j2 * 4096 + c * 128: j2 * 4096 + (c + 1) * 128],
                    identb[:, :]), waits=w, inc="txp")
                p.emit("vector", lambda v, c=c, j2=j2, bank=bank: v.tensor_copy(
                    keepT[:, c * 256 + j2 * 128: c * 256 + j2 * 128 + 128], bank[:, 0:128]),
                    waits=[("txp", p.c("txp"))], inc="tvx")

        for h in range(32):
            kb0 = 0
            for g2 in range(nch // 2):
                m_, j_ = g2 % 4, g2 // 4
                p.dma("sync", kb_sb[:, g2 * 256:(g2 + 1) * 256],
                      ex_k1[m_ * 4096 + h * 128: m_ * 4096 + (h + 1) * 128,
                            256 * j_:256 * j_ + 256],
                      waits=[("tc", p.c("tc")), ("cc", 6)], inc="ldk")
                p.dma("sync", vb_sb[:, g2 * 256:(g2 + 1) * 256].rearrange(
                          "p (e v) -> p e v", e=2),
                      ex_v1[m_ * 1024 + 256 * j_: m_ * 1024 + 256 * j_ + 256,
                            h * 128:(h + 1) * 128].rearrange("(e p) v -> p e v", p=128),
                      waits=[("tc", p.c("tc")), ("cc", 6)], inc="ldk")
            for kc in range(nch):
                pa = ps[kc % 2]

                def _mm(t, h=h, kc=kc, pa=pa, kb0=kb0):
                    t.matmul(pa[:, 0:256], kb_sb[:, kb0 + kc * 128: kb0 + kc * 128 + 128],
                             q1n[:, h * 256:(h + 1) * 256], start=True, stop=False)
                    return t.matmul(pa[:, 0:256], kpe_sb[:, kc * 128:(kc + 1) * 128],
                                    q1p[:, h * 256:(h + 1) * 256], start=False, stop=True)
                p.emit("tensor", _mm, waits=[("ldk", p.c("ldk")), ("ldg", p.c("ldg")),
                                             ("scA", p.c("scA") - 1)], inc="tc")
                p.emit("scalar", lambda s, kc=kc, pa=pa: s.activation(
                    pt[:, kc * 256:(kc + 1) * 256], pa[:, 0:256], AF.Exp,
                    bias=mexpb[:, 0:1], scale=SC1),
                    waits=[("tc", p.c("tc")), ("vc", 6)], inc="scA")
            p.emit("vector", lambda v, nch=nch: v.tensor_tensor(
                pt[:, 0:nch * 256], pt[:, 0:nch * 256], keepT[:, 0:nch * 256], OP.mult),
                waits=[("scA", p.c("scA")), ("tvx", p.c("tvx"))], inc="vs")
            def _sums(t, nch=nch):
                ins = None
                for kc in range(nch):
                    ins = t.matmul(psum1[0:1, 0:256], onescol[:, :],
                                   pt[:, kc * 256:(kc + 1) * 256],
                                   start=(kc == 0), stop=(kc == nch - 1))
                return ins
            p.emit("tensor", _sums, waits=[("vs", p.c("vs"))], inc="tc")
            p.emit("vector", lambda v: v.tensor_copy(rowsf[0:1, 0:256], psum1[0:1, 0:256]),
                   waits=[("tc", p.c("tc"))], inc="vs")
            if dbg_sums is not None:
                p.dma("sync", dbg_sums[gi, h], rowsf[0:1, 0:256], waits=[("vs", p.c("vs"))], inc="st")
            p.emit("vector", lambda v: v.reciprocal(rowsf[0:1, 256:512], rowsf[0:1, 0:256]),
                   waits=[("vs", p.c("vs"))], inc="vs")
            p.emit("tensor", lambda t: t.matmul(psm[:, 0:256], onesf[0:1, 0:128],
                                                rowsf[0:1, 256:512], start=True, stop=True),
                   waits=[("vs", p.c("vs"))], inc="tc")
            p.emit("vector", lambda v: v.tensor_copy(recipb[:, :], psm[:, 0:256]),
                   waits=[("tc", p.c("tc"))], inc="vs")
            pb = ps[2 + h % 2]

            def _pv(t, h=h, pb=pb, nch=nch, kb0=kb0):
                ins = None
                for kc in range(nch):
                    ins = t.matmul(pb[:, 0:256], vb_sb[:, kb0 + kc * 128: kb0 + kc * 128 + 128],
                                   pt[:, kc * 256:(kc + 1) * 256],
                                   start=(kc == 0), stop=(kc == nch - 1))
                return ins
            p.emit("tensor", _pv, waits=[("vs", p.c("vs")), ("vpv", p.c("vpv") - 1)], inc="tc")
            p.emit("vector", lambda v, h=h, pb=pb: v.tensor_tensor(
                attnfm[:, h * 256:(h + 1) * 256], pb[:, 0:256], recipb[:, :], OP.mult),
                waits=[("tc", p.c("tc"))], inc="vpv")

        if dbg_attn is not None:
            p.dma("sync", dbg_attn[gi], attnfm[:, :], waits=[("vpv", p.c("vpv"))], inc="st")
        for ob in range(16):
            p.dma(g, wbuf[:, :].rearrange("p (c n) -> p c n", c=NCD),
                  ow_d[:, ob * 256:(ob + 1) * 256].rearrange("(c p) n -> p c n", p=128),
                  waits=[("tc", p.c("tc"))], inc="ldw")
            for j2 in range(2):
                j = 2 * gi + j2
                pa = ps[j2 % 2]
                p.dma("sync", stage2[:, 0:256], x2part[j * 128:(j + 1) * 128, ob * 256:(ob + 1) * 256],
                      waits=[("vo", p.c("vo"))], inc="ldx")

                def _mm(t, j2=j2, pa=pa):
                    ins = None
                    for c in range(NCD):
                        ins = t.matmul(pa[:, 0:256],
                                       attnfm[:, c * 256 + j2 * 128: c * 256 + j2 * 128 + 128],
                                       wbuf[:, c * 256:(c + 1) * 256],
                                       start=(c == 0), stop=(c == NCD - 1))
                    return ins
                p.emit("tensor", _mm, waits=[("ldw", p.c("ldw")), ("vpv", p.c("vpv")),
                                             ("vo", p.c("vo"))], inc="tc")
                p.emit("vector", lambda v, pa=pa: v.scalar_tensor_tensor(
                    stage[:, 0:256], pa[:, 0:256], 1.0, stage2[:, 0:256], OP.mult, OP.add),
                    waits=[("tc", p.c("tc")), ("ldx", p.c("ldx")), ("stx", p.c("stx"))],
                    inc="vo")
                p.dma("sync", out_d[j * 128:(j + 1) * 128, ob * 256:(ob + 1) * 256],
                      stage[:, 0:256], waits=[("vo", p.c("vo"))], inc="stx")

    p.emit("sync", lambda e: e.nop(), waits=[("stx", p.c("stx")), ("st", p.c("st"))])


def build_fused():
    nc = new_nc()
    es = ExitStack()

    def din(n, sh, dt=F32):
        return nc.dram_tensor(n, sh, dt, kind="ExternalInput").ap()

    X = {}
    X["x0"] = din("x0", [TOK, D])
    X["wqkv"] = din("wqkv", [D, 3072])
    X["bqkv"] = din("bqkv", [1, 3072])
    X["identb_d"] = din("identb", [128, 128], BF16)
    for n, sh in [("wo", [1024, D]), ("bo", [1, D]), ("kern", [D, D]),
                  ("bias0", [1, D]), ("acomb", [D, 544]), ("bcomb", [544, D]),
                  ("gate_sel", [1, 128]), ("qa", [D, 1536]), ("qbn", [1536, 4096]),
                  ("qbp", [1536, 2048]), ("wq_row", [1, 1536]), ("kva", [D, 576]),
                  ("kvbk", [512, 4096]), ("kvbv", [512, 4096]), ("wkv_row", [1, 512]),
                  ("combow", [D, 32]), ("idxwq", [D, 2048]), ("idxwk", [D, 128]),
                  ("bthr", [1, 1]), ("wbaset", [D, D]), ("bbase", [1, D]),
                  ("acat", [D, 128]), ("bcat", [128, D]), ("ropec", [TOK, 32]),
                  ("ropes", [TOK, 32]), ("ow", [D, D])]:
        X[n] = din(n, sh)
    X["cmasks"] = din("cmasks", [NT, 128, 4096], BF16)

    ints = {}
    for n, sh, dt in [("xn0fm_d", [D, TOK], BF16), ("q0fm_d", [1024, TOK], BF16),
                      ("k0fm_d", [1024, TOK], BF16), ("v0tm_d", [TOK, 1024], BF16),
                      ("ex_k0", [4 * 1024, TOK], BF16), ("ex_v0", [4 * TOK, 1024], BF16),
                      ("x1part", [TOK, D], F32), ("x2p_d", [TOK, D], F32),
                      ("q1nfm_d", [4096, TOK], BF16), ("q1pe_d", [TOK, 2048], BF16),
                      ("q1pT_d", [2048, TOK], BF16), ("k1nfm_d", [4096, TOK], BF16),
                      ("k1pe_d", [TOK, 64], BF16), ("kpeT_d", [64, TOK], BF16),
                      ("v1tm_d", [TOK, 4096], BF16), ("ikfm_d", [128, TOK], BF16),
                      ("iqfm_d", [2048, TOK], BF16), ("iw_d", [TOK, 16], F32),
                      ("ex_k1", [4 * 4096, TOK], BF16), ("ex_v1", [4 * TOK, 4096], BF16),
                      ("ex_kpe", [4 * 64, TOK], BF16), ("ex_ik", [4 * 128, TOK], BF16)]:
        h = nc.dram_tensor(n, sh, dt, kind="Internal")
        ints[n] = h
        X[n] = h.ap()
    X["out_d"] = nc.dram_tensor("out", [TOK, D], F32, kind="ExternalOutput").ap()

    X["ps"] = [es.enter_context(nc.psum_tensor(f"ps{i}", [128, 512], F32))
               for i in range(4)]
    X["pst1"] = es.enter_context(nc.psum_tensor("pst1", [128, 1024], BF16))
    X["pst2"] = es.enter_context(nc.psum_tensor("pst2", [128, 1024], BF16))
    X["psum1"] = es.enter_context(nc.psum_tensor("psum1", [128, 512], F32))
    X["psm"] = es.enter_context(nc.psum_tensor("psm", [128, 512], F32))

    mark = es.enter_context(nc.sbuf_tensor("arena_mark", [128, 32], F32))
    base = nc.lookup_mloc(mark).addr + 128

    p = Prog(nc, es)
    emit_phase1(nc, es, p, Arena(nc, "p1", base), X)
    emit_cc(p, ints["k0fm_d"], ints["ex_k0"], waits=[("st", p.c("st"))])
    emit_cc(p, ints["v0tm_d"], ints["ex_v0"], waits=[("st", p.c("st"))])
    barrier(p)
    emit_phase2(nc, es, p, Arena(nc, "p2", base), X)
    ccw = [("stx", p.c("stx")), ("stq", p.c("stq")), ("st", p.c("st"))]
    for s_, d_ in [("k1nfm_d", "ex_k1"), ("v1tm_d", "ex_v1"),
                   ("kpeT_d", "ex_kpe"), ("ikfm_d", "ex_ik")]:
        emit_cc(p, ints[s_], ints[d_], waits=ccw)
    barrier(p)
    emit_phase3(nc, es, p, Arena(nc, "p3", base), X)
    if DBG_DUMP:
        snap = [(s, p.c(s)) for s in list(p.cnt.keys()) if p.c(s) > 0]
        for n in ["xn0fm_d", "q0fm_d", "k0fm_d", "v0tm_d", "ex_k0", "ex_v0",
                  "x1part", "x2p_d", "q1nfm_d", "q1pe_d", "q1pT_d", "k1nfm_d",
                  "k1pe_d", "kpeT_d", "v1tm_d", "ikfm_d", "iqfm_d", "iw_d",
                  "ex_k1", "ex_v1", "ex_kpe", "ex_ik"]:
            h = ints[n]
            dd = nc.dram_tensor("dbg_" + n, list(h.shape), h.dtype,
                                kind="ExternalOutput").ap()
            p.dma("sync", dd, h.ap(), waits=snap, inc="st")
        p.emit("sync", lambda e: e.nop(), waits=[("st", p.c("st"))])
    p.build()
    es.close()
    return nc


_NC_CACHE = {}


def _get_nc(name, builder):
    if name not in _NC_CACHE:
        _NC_CACHE[name] = builder()
    return _NC_CACHE[name]


def _own_sidx(m):
    u = np.arange(TOK)
    return 256 * (m + 4 * (u // 256)) + u % 256


def make_cmasks(m):
    cm = np.zeros((NT, 128, 4096), np.float32)
    for j in range(NT):
        T = 2 * m + 8 * (j // 2) + (j % 2)
        kext = nchpad_tile(j) * 128
        rows = 128 * T + np.arange(128)
        cols = np.arange(kext)
        cm[j, :, :kext] = np.where(cols[None, :] <= rows[:, None], 0.0, NEG)
        cm[j, :, kext:] = 0.0
    return cm.astype(bf)


def kernel(**inp):
    f32 = np.float32
    X_ = np.asarray(inp["L0_attn_x"], f32)
    identb = np.eye(128, dtype=bf)
    sidx = [_own_sidx(m) for m in range(4)]

    lA = np.asarray(inp["L0_ffn_lora_A"], f32)
    lB = np.asarray(inp["L0_ffn_lora_B"], f32)
    lAs = np.asarray(inp["L0_ffn_lora_A_shared"], f32)
    lBs = np.asarray(inp["L0_ffn_lora_B_shared"], f32)
    acomb = np.ascontiguousarray(np.concatenate(
        [lA.transpose(2, 0, 1).reshape(D, 512), lAs.T], axis=1))
    bcomb = np.ascontiguousarray(np.concatenate(
        [lB.transpose(0, 2, 1).reshape(512, D), lBs.T], axis=0))
    gate = np.asarray(inp["L0_ffn_gate"], f32)
    qb = np.asarray(inp["L1_attn_q_b_w"], f32).reshape(1536, 32, 192)
    qbn = np.ascontiguousarray(qb[:, :, :128].reshape(1536, 4096))
    qbp = np.ascontiguousarray(qb[:, :, 128:].reshape(1536, 2048))
    kvb = np.asarray(inp["L1_attn_kv_b_w"], f32).reshape(512, 32, 256)
    kvbk = np.ascontiguousarray(kvb[:, :, :128].reshape(512, 4096))
    kvbv = np.ascontiguousarray(kvb[:, :, 128:].reshape(512, 4096))
    combow = np.zeros((D, 32), f32)
    combow[:, 0:8] = np.asarray(inp["L1_ffn_W_gate"], f32).T
    combow[:, 8] = np.asarray(inp["L1_ffn_W_thr"], f32)
    combow[:, 9:25] = np.asarray(inp["L1_attn_idx_weights_w"], f32)
    acat = np.ascontiguousarray(
        np.asarray(inp["L1_ffn_A_w"], f32).transpose(2, 0, 1).reshape(D, 128))
    bcat = np.ascontiguousarray(
        np.asarray(inp["L1_ffn_B_w"], f32).transpose(0, 2, 1).reshape(128, D))
    inv = 1.0 / 10000.0 ** (np.arange(0, 64, 2, dtype=np.float64) / 64)
    posf = np.arange(4096, dtype=np.float64)
    ang = posf[:, None] * inv[None, :]
    cosT, sinT = np.cos(ang).astype(f32), np.sin(ang).astype(f32)

    nc = _get_nc("fused", build_fused)
    maps = []
    for c in range(8):
        b, m = c // 4, c % 4
        gs = np.zeros((1, 128), f32)
        for j in range(NT):
            gs[0, j * 16:(j + 1) * 16] = gate[b * 4 + (j // 2)]
        maps.append({
            "x0": np.ascontiguousarray(X_[b, sidx[m], :]),
            "wqkv": np.asarray(inp["L0_attn_Wqkv"], f32),
            "bqkv": np.asarray(inp["L0_attn_bqkv"], f32).reshape(1, 3072),
            "identb": identb,
            "wo": np.asarray(inp["L0_attn_Wo"], f32),
            "bo": np.asarray(inp["L0_attn_bo"], f32).reshape(1, D),
            "kern": np.asarray(inp["L0_ffn_kernel"], f32),
            "bias0": np.asarray(inp["L0_ffn_bias"], f32).reshape(1, D),
            "acomb": acomb, "bcomb": bcomb, "gate_sel": gs,
            "qa": np.asarray(inp["L1_attn_q_a_w"], f32),
            "qbn": qbn, "qbp": qbp,
            "wq_row": np.asarray(inp["L1_attn_q_a_norm_w"], f32).reshape(1, 1536),
            "kva": np.asarray(inp["L1_attn_kv_a_w"], f32),
            "kvbk": kvbk, "kvbv": kvbv,
            "wkv_row": np.asarray(inp["L1_attn_kv_a_norm_w"], f32).reshape(1, 512),
            "combow": combow,
            "idxwq": np.asarray(inp["L1_attn_idx_wq_w"], f32),
            "idxwk": np.asarray(inp["L1_attn_idx_wk_w"], f32),
            "bthr": np.asarray(inp["L1_ffn_b_thr"], f32).reshape(1, 1),
            "wbaset": np.ascontiguousarray(np.asarray(inp["L1_ffn_W_base"], f32).T),
            "bbase": np.asarray(inp["L1_ffn_b_base"], f32).reshape(1, D),
            "acat": acat, "bcat": bcat,
            "ropec": np.ascontiguousarray(cosT[sidx[m], :]),
            "ropes": np.ascontiguousarray(sinT[sidx[m], :]),
            "cmasks": make_cmasks(m),
            "ow": np.asarray(inp["L1_attn_o_w"], f32),
        })
    r = run_bass_kernel_spmd(nc, maps, core_ids=list(range(8))).results

    out = np.empty((2, 4096, D), f32)
    for c in range(8):
        b, m = c // 4, c % 4
        out[b, sidx[m], :] = np.asarray(r[c]["out"], f32)
    return out


# version 04 (final)
# speedup vs baseline: 1.0218x; 1.0218x over previous
import numpy as np
import ml_dtypes
from contextlib import ExitStack
import concourse.bass as bass
import concourse.mybir as mybir
from concourse.bass_utils import run_bass_kernel_spmd

F32 = mybir.dt.float32
BF16 = mybir.dt.bfloat16
AX = mybir.AxisListType.X
OP = mybir.AluOpType
AF = mybir.ActivationFunctionType
bf = ml_dtypes.bfloat16

D = 4096
TOK = 1024
NT = 8
NG = 4
NCD = D // 128
MEXP = 12.0
NEG = -1.0e30


def nchpad_tile(j):
    return 7 + 8 * (j // 2) + (j % 2)


def nchpad_group(g):
    return 8 + 8 * g


class Prog:
    ENGS = ("tensor", "vector", "scalar", "gpsimd", "sync")

    def __init__(self, nc, es):
        self.nc = nc
        self.es = es
        self.seq = {e: [] for e in self.ENGS}
        self.sems = {}
        self.cnt = {}

    def sem(self, name):
        if name not in self.sems:
            self.sems[name] = self.es.enter_context(self.nc.semaphore(name))
            self.cnt[name] = 0
        return self.sems[name]

    def c(self, name):
        return self.cnt.get(name, 0)

    def emit(self, eng, fn, waits=(), inc=None, incn=1):
        wv = []
        for w in waits:
            if isinstance(w, tuple):
                s, v = w
            else:
                s, v = w, self.c(w)
            self.sem(s)
            if v > 0:
                wv.append((s, v))
        tgt = None
        if inc is not None:
            self.sem(inc)
            self.cnt[inc] = self.c(inc) + incn
            tgt = (inc, incn)
        self.seq[eng].append((fn, wv, tgt))

    def dma(self, eng, out, in_, waits=(), inc="ld", **kw):
        self.emit(eng, lambda e: e.dma_start(out=out, in_=in_, **kw),
                  waits=waits, inc=inc, incn=16)

    def build(self):
        nc = self.nc
        blk = self.es.enter_context(nc.Block())
        for eng_name in self.ENGS:
            seq = self.seq[eng_name]

            def body(e, seq=seq):
                lastw = {}
                for fn, wv, tgt in seq:
                    for s, v in wv:
                        if lastw.get(s, 0) < v:
                            e.wait_ge(self.sems[s], v)
                            lastw[s] = v
                    ins = fn(e)
                    if tgt is not None:
                        ins.then_inc(self.sems[tgt[0]], tgt[1])
            getattr(blk, eng_name)(body)


def new_nc():
    return bass.Bass("TRN2", target_bir_lowering=False, debug=False,
                     detect_race_conditions=False)


RG = [[0, 1, 2, 3], [4, 5, 6, 7]]
DBG_DUMP = False


class Arena:
    def __init__(self, nc, tag, base):
        self.nc, self.tag, self.base, self.addr = nc, tag, base, 0

    def sb(self, name, shape, dtype):
        per = int(np.prod(shape[1:])) * mybir.dt.size(dtype)
        a = (self.addr + 31) & ~31
        self.addr = a + per
        assert self.base + self.addr <= 229375, (self.tag, self.addr)
        return self.nc.alloc_sbuf_tensor_at(
            f"{name}_{self.tag}", shape, dtype, offset=self.base + a)


def barrier(p, tag="ph"):
    snap = [(s, p.c(s)) for s in list(p.cnt.keys())
            if p.c(s) > 0 and not s.startswith("cc")]
    for eng in Prog.ENGS:
        p.emit(eng, lambda e: e.nop(), waits=snap, inc=tag)
    n = p.c(tag)
    for eng in Prog.ENGS:
        p.emit(eng, lambda e: e.nop(), waits=[(tag, n)])


def emit_cc(p, src_t, dst_t, waits, sem="cc"):
    p.emit("gpsimd", lambda g_: g_.collective_compute(
        "AllGather", OP.bypass, replica_groups=RG,
        ins=[src_t.ap().opt()], outs=[dst_t.ap().opt()]), waits=waits, inc=sem)


def emit_ln(p, x_sb, xn_sb, stats, junk, eps=1e-5, dim=D):
    p.emit("vector", lambda v: v.reduce_sum(stats[:, 0:1], x_sb, AX), inc="vc")
    p.emit("scalar", lambda s: s.activation(junk, x_sb, AF.Square,
                                            accum_out=stats[:, 1:2]),
           waits=[("vc", p.c("vc"))], inc="sc")

    def _m(v):
        v.tensor_scalar(stats[:, 0:1], stats[:, 0:1], 1.0 / dim, None, OP.mult)
        v.scalar_tensor_tensor(stats[:, 2:3], stats[:, 0:1], 1.0, stats[:, 0:1],
                               OP.mult, OP.mult)
        v.tensor_scalar(stats[:, 1:2], stats[:, 1:2], 1.0 / dim, None, OP.mult)
        v.tensor_tensor(stats[:, 1:2], stats[:, 1:2], stats[:, 2:3], OP.subtract)
        return v.tensor_scalar(stats[:, 1:2], stats[:, 1:2], eps, None, OP.add)
    p.emit("vector", _m, waits=[("sc", p.c("sc"))], inc="vc")
    p.emit("scalar", lambda s: s.activation(stats[:, 3:4], stats[:, 1:2], AF.Ln),
           waits=("vc",), inc="sc")
    p.emit("scalar", lambda s: s.activation(stats[:, 1:2], stats[:, 3:4], AF.Exp,
                                            scale=-0.5), inc="sc")
    p.emit("vector", lambda v: v.scalar_tensor_tensor(
        stats[:, 0:1], stats[:, 0:1], -1.0, stats[:, 1:2], OP.mult, OP.bypass),
        waits=("sc",), inc="vc")
    p.emit("vector", lambda v: v.tensor_scalar(
        xn_sb, x_sb, stats[:, 0:1], stats[:, 1:2], OP.add, OP.mult),
        waits=[("txp", p.c("txp")), ("vc", p.c("vc"))], inc="sc")




def emit_transpose_tile(p, src_tm_fn, dst_fm_fn, identb, banks, nch, extra_waits=(),
                        rows_fn=None):
    base = p.c("tvx")
    for c in range(nch):
        bank = banks[(base + c) % 2]
        rows = 128 if rows_fn is None else rows_fn(c)
        w = list(extra_waits)
        if base + c - 1 > 0:
            w.append(("tvx", base + c - 1))
        p.emit("tensor", lambda t, c=c, bank=bank, rows=rows: t.transpose(
            bank[0:rows, 0:128], src_tm_fn(c), identb), waits=w, inc="txp")
        p.emit("vector", lambda v, c=c, bank=bank, rows=rows: v.tensor_copy(
            dst_fm_fn(c), bank[0:rows, 0:128]), waits=("txp",), inc="tvx")


def emit_rope_block(p, pa, j, nheads, cossb, sinsb, rtmp, rtmp2, outb):
    n = nheads
    cosv = cossb[:, j * 32:(j + 1) * 32].unsqueeze(1).to_broadcast([128, n, 32])
    sinv = sinsb[:, j * 32:(j + 1) * 32].unsqueeze(1).to_broadcast([128, n, 32])
    src = pa[:, 0:n * 64].rearrange("p (h r two) -> p h r two", h=n, two=2)
    dst = outb[:, 0:n * 64].rearrange("p (h r two) -> p h r two", h=n, two=2)
    t1 = rtmp[:, 0:n * 32].rearrange("p (h r) -> p h r", h=n)
    t2 = rtmp2[:, 0:n * 32].rearrange("p (h r) -> p h r", h=n)

    def _r(v):
        v.tensor_tensor(t1, src[:, :, :, 0], cosv, OP.mult)
        v.tensor_tensor(t2, src[:, :, :, 1], sinv, OP.mult)
        v.tensor_tensor(dst[:, :, :, 0], t1, t2, OP.subtract)
        v.tensor_tensor(t1, src[:, :, :, 1], cosv, OP.mult)
        v.tensor_tensor(t2, src[:, :, :, 0], sinv, OP.mult)
        return v.tensor_tensor(dst[:, :, :, 1], t1, t2, OP.add)
    p.emit("vector", _r, waits=[("tc", p.c("tc")), ("stx", p.c("stx"))], inc="vpe")


def emit_phase1(nc, es, p, A, X):
    x0 = X["x0"]; wqkv = X["wqkv"]; bqkv = X["bqkv"]; identb_d = X["identb_d"]
    xn0fm_d = X["xn0fm_d"]; q0fm_d = X["q0fm_d"]
    k0fm_d = X["k0fm_d"]; v0tm_d = X["v0tm_d"]

    xnfm = A.sb("xnfm", [128, NCD * TOK], BF16)
    xtile = A.sb("xtile", [128, D], F32)
    xntile = A.sb("xntile", [128, D], BF16)
    junk = A.sb("junk", [128, D], BF16)
    stats = A.sb("stats", [128, 8], F32)
    wbuf = A.sb("wbuf", [128, NCD * 512], BF16)
    brow = A.sb("brow", [1, 3072], BF16)
    ones1 = A.sb("ones1", [1, 512], BF16)
    identb = A.sb("identb_sb", [128, 128], BF16)
    stage = A.sb("stage", [128, 512], BF16)
    pst1 = X["pst1"]; pst2 = X["pst2"]; pa = X["ps"][0]; pb = X["ps"][1]

    p.dma("gpsimd", identb[:, :], identb_d)
    p.dma("gpsimd", brow[:, :], bqkv)
    p.emit("vector", lambda v: v.memset(ones1[:, :], 1.0), inc="vc")

    for j in range(NT):
        p.dma("gpsimd", xtile[:, :], x0[j * 128:(j + 1) * 128, :],
              waits=[("sc", p.c("sc"))], inc="ldx")
        p.emit("vector", lambda v: v.tensor_copy(stats[:, 4:5], stats[:, 4:5]) if False else v.memset(stats[:, 4:5], 0.0),
               waits=[("ldx", p.c("ldx"))], inc="vc")
        emit_ln(p, xtile[:, :], xntile[:, :], stats, junk[:, :])
        emit_transpose_tile(
            p, lambda c: xntile[:, c * 128:(c + 1) * 128],
            lambda c, j=j: xnfm[:, c * TOK + j * 128: c * TOK + j * 128 + 128],
            identb[:, :], (pst1, pst2), NCD, extra_waits=[("sc", p.c("sc"))])
    for c in range(NCD):
        p.dma("sync", xn0fm_d[c * 128:(c + 1) * 128, :], xnfm[:, c * TOK:(c + 1) * TOK],
              waits=[("tvx", p.c("tvx"))], inc="st")

    pss = [pa, pb]
    pi = [0]

    def fm_out_block(col0, out_d, row0):
        p.dma("gpsimd", wbuf[:, :].rearrange("p (c n) -> p c n", c=NCD),
              wqkv[:, col0:col0 + 512].rearrange("(c p) n -> p c n", p=128),
              waits=[("tc", p.c("tc"))], inc="ldw")
        for fsub in range(4):
            for st_ in range(2):
                ps = pss[pi[0] % 2]
                pi[0] += 1

                def _mm(t, ps=ps, fsub=fsub, st_=st_, col0=col0):
                    t.matmul(ps[:, 0:512], brow[0:1, col0 + fsub * 128: col0 + fsub * 128 + 128],
                             ones1[0:1, :], start=True, stop=False)
                    ins = None
                    for c in range(NCD):
                        ins = t.matmul(ps[:, 0:512],
                                       wbuf[:, c * 512 + fsub * 128: c * 512 + fsub * 128 + 128],
                                       xnfm[:, c * TOK + st_ * 512: c * TOK + st_ * 512 + 512],
                                       start=False, stop=(c == NCD - 1))
                    return ins
                p.emit("tensor", _mm,
                       waits=[("ldw", p.c("ldw")), ("tvx", p.c("tvx")),
                              ("ld", p.c("ld")), ("vc", 1), ("vpe", p.c("vpe"))],
                       inc="tc")
                p.emit("vector", lambda v, ps=ps: v.tensor_copy(stage[:, :], ps[:, 0:512]),
                       waits=[("tc", p.c("tc")), ("st", p.c("st"))], inc="vpe")
                p.dma("sync", out_d[col0 - row0 + fsub * 128: col0 - row0 + fsub * 128 + 128,
                                    st_ * 512:(st_ + 1) * 512],
                      stage[:, :], waits=[("vpe", p.c("vpe"))], inc="st")

    def tm_out_block(col0, out_d, ocol0):
        p.dma("gpsimd", wbuf[:, :].rearrange("p (c n) -> p c n", c=NCD),
              wqkv[:, col0:col0 + 512].rearrange("(c p) n -> p c n", p=128),
              waits=[("tc", p.c("tc"))], inc="ldw")
        for j in range(NT):
            ps = pss[pi[0] % 2]
            pi[0] += 1

            def _mm(t, ps=ps, j=j, col0=col0):
                t.matmul(ps[:, 0:512], ones1[0:1, 0:128], brow[0:1, col0:col0 + 512],
                         start=True, stop=False)
                ins = None
                for c in range(NCD):
                    ins = t.matmul(ps[:, 0:512],
                                   xnfm[:, c * TOK + j * 128: c * TOK + j * 128 + 128],
                                   wbuf[:, c * 512:(c + 1) * 512],
                                   start=False, stop=(c == NCD - 1))
                return ins
            p.emit("tensor", _mm,
                   waits=[("ldw", p.c("ldw")), ("tvx", p.c("tvx")),
                          ("ld", p.c("ld")), ("vc", 1), ("vpe", p.c("vpe"))],
                   inc="tc")
            p.emit("vector", lambda v, ps=ps: v.tensor_copy(stage[:, :], ps[:, 0:512]),
                   waits=[("tc", p.c("tc")), ("st", p.c("st"))], inc="vpe")
            p.dma("sync", out_d[j * 128:(j + 1) * 128, col0 - ocol0: col0 - ocol0 + 512],
                  stage[:, :], waits=[("vpe", p.c("vpe"))], inc="st")

    for blk in range(2):
        fm_out_block(blk * 512, q0fm_d, 0)
    for blk in range(2):
        fm_out_block(1024 + blk * 512, k0fm_d, 1024)
    for blk in range(2):
        tm_out_block(2048 + blk * 512, v0tm_d, 2048)



SC0 = 1.0 / 32.0


def emit_phase2(nc, es, p, A, X):
    x0 = X["x0"]; wo = X["wo"]; bo = X["bo"]; kern = X["kern"]; bias0 = X["bias0"]
    acomb = X["acomb"]; bcomb = X["bcomb"]; gate_sel = X["gate_sel"]
    qa = X["qa"]; qbn = X["qbn"]; qbp = X["qbp"]; wq_row = X["wq_row"]
    kva = X["kva"]; kvbk = X["kvbk"]; kvbv = X["kvbv"]; wkv_row = X["wkv_row"]
    combow = X["combow"]; idx_wq_d = X["idxwq"]; idx_wk_d = X["idxwk"]
    bthr = X["bthr"]; wbaset = X["wbaset"]; bbase = X["bbase"]
    acat = X["acat"]; bcat_d = X["bcat"]; ropec = X["ropec"]; ropes = X["ropes"]
    xn0fm_d = X["xn0fm_d"]; identb_d = X["identb_d"]
    x2p_d = X["x2p_d"]; q1nfm_d = X["q1nfm_d"]; q1pe_d = X["q1pe_d"]
    k1nfm_d = X["k1nfm_d"]; k1pe_d = X["k1pe_d"]; v1tm_d = X["v1tm_d"]
    ikfm_d = X["ikfm_d"]; iqfm_d = X["iqfm_d"]; iw_d = X["iw_d"]
    x1part = X["x1part"]; ex_k0 = X["ex_k0"]; ex_v0 = X["ex_v0"]
    q0fm_d = X["q0fm_d"]
    dbg_qlat = dbg_combo = dbg_hgfm = dbg_latfm = dbg_wn = dbg_st = dbg_h = None

    arena = A.sb("arena", [128, 32768], BF16)
    wbuf = A.sb("wbuf", [128, 16384], BF16)
    r3 = A.sb("r3", [128, 16384], BF16)
    r4 = A.sb("r4", [128, 12800], BF16)
    r5 = A.sb("r5", [128, 8192], BF16)
    stage = A.sb("stage", [128, 512], F32)
    stage2 = A.sb("stage2", [128, 512], F32)
    stageb = A.sb("stageb", [128, 512], BF16)
    hgbuf = A.sb("hgbuf", [128, 128], BF16)
    stats = A.sb("stats", [128, 16], F32)
    brow2 = A.sb("brow2", [1, 512], BF16)
    rows = A.sb("rows", [1, 2048], BF16)
    rowsf = A.sb("rowsf", [1, 512], F32)
    ones1 = A.sb("ones1", [1, 512], BF16)
    onesf = A.sb("onesf", [1, 128], F32)
    onescol = A.sb("onescol", [128, 1], BF16)
    identb = A.sb("identb_sb", [128, 128], BF16)
    cossb = A.sb("cossb", [128, 256], F32)
    sinsb = A.sb("sinsb", [128, 256], F32)
    gb = A.sb("gb", [128, 128], F32)
    wqb = A.sb("wqb", [128, 1536], BF16)
    wkvb = A.sb("wkvb", [128, 512], BF16)
    bthrb = A.sb("bthrb", [128, 1], F32)
    mexpb = A.sb("mexpb", [128, 1], F32)
    combo = A.sb("combo", [128, 8 * 32], F32)
    gatew = A.sb("gatew", [128, 16], F32)
    hgfm = A.sb("hgfm", [128, 8 * 128], BF16)
    bcatb = A.sb("bcatb", [128, 4096], BF16)
    abuf2 = A.sb("abuf2", [128, 32 * 32], BF16)
    bbuf2 = A.sb("bbuf2", [32, 512], BF16)
    rtmp = A.sb("rtmp", [128, 256], F32)
    rtmp2 = A.sb("rtmp2", [128, 256], F32)

    ps = X["ps"]; pst1 = X["pst1"]; pst2 = X["pst2"]
    psum1 = X["psum1"]; psm = X["psm"]

    xn0fm = arena
    v0 = arena
    xn1fm = arena
    r3f = r3.bitcast(F32)
    r4f = r4.bitcast(F32)
    g = "gpsimd"

    p.dma(g, identb[:, :], identb_d)
    p.dma(g, cossb[:, :].rearrange("p (c n) -> p c n", c=8),
          ropec.rearrange("(c p) n -> p c n", p=128))
    p.dma(g, sinsb[:, :].rearrange("p (c n) -> p c n", c=8),
          ropes.rearrange("(c p) n -> p c n", p=128))
    p.dma(g, rows[0:1, 0:128], gate_sel)
    p.dma(g, rows[0:1, 128:128 + 1536], wq_row)
    p.dma(g, rowsf[0:1, 0:1], bthr)
    p.emit("vector", lambda v: v.memset(ones1[:, :], 1.0), inc="vc")
    p.emit("vector", lambda v: v.memset(onesf[:, :], 1.0), inc="vc")
    p.emit("vector", lambda v: v.memset(onescol[:, :], 1.0), inc="vc")
    p.emit("vector", lambda v: v.memset(mexpb[:, :], -MEXP), inc="vc")

    def bc128(dst, src_row, n, f32src, bank=psm):
        p.emit("tensor", lambda t: t.matmul(
            bank[:, 0:n], (onesf if f32src else ones1)[0:1, 0:128], src_row,
            start=True, stop=True),
            waits=[("ld", p.c("ld")), ("vc", p.c("vc")), ("vpe", p.c("vpe"))], inc="tc")
        p.emit("vector", lambda v: v.tensor_copy(dst, bank[:, 0:n]),
               waits=[("tc", p.c("tc"))], inc="vpe")

    bc128(gb[:, :], rows[0:1, 0:128], 128, False)
    for i in range(3):
        bc128(wqb[:, i * 512:(i + 1) * 512], rows[0:1, 128 + i * 512:128 + (i + 1) * 512], 512, False)
    bc128(bthrb[:, :], rowsf[0:1, 0:1], 1, True)

    p.dma(g, brow2[0:1, 0:512], wkv_row, waits=[("vpe", p.c("vpe"))], inc="ld")
    bc128(wkvb[:, :], brow2[0:1, 0:512], 512, False)

    p.dma(g, xn0fm[:, :].rearrange("p (c n) -> p c n", c=NCD),
          xn0fm_d.rearrange("(c p) n -> p c n", p=128), inc="ldA")

    p.dma(g, wbuf[:, 0:NCD * 512].rearrange("p (c n) -> p c n", c=NCD),
          acomb[:, 0:512].rearrange("(c p) n -> p c n", p=128), inc="ldw")
    for q_ in range(4):
        p.dma(g, abuf2[:, q_ * 256:(q_ + 1) * 256].rearrange("p (c n) -> p c n", c=8),
              acomb[q_ * 1024:(q_ + 1) * 1024, 512:544].rearrange("(c p) n -> p c n", p=128),
              inc="ldw")
    axfm = r4

    for j in range(NT):
        pa = ps[j % 2]

        def _mm(t, j=j, pa=pa):
            ins = None
            for c in range(NCD):
                ins = t.matmul(pa[:, 0:512], xn0fm[:, c * TOK + j * 128: c * TOK + j * 128 + 128],
                               wbuf[:, c * 512:(c + 1) * 512], start=(c == 0), stop=(c == NCD - 1))
            return ins
        p.emit("tensor", _mm, waits=[("ldw", p.c("ldw")), ("ldA", p.c("ldA")),
                                     ("vpe", p.c("vpe")), ("vc", p.c("vc"))], inc="tc")
        def _ev(v, j=j, pa=pa):
            ins = None
            for n in range(16):
                ins = v.tensor_scalar(r3[:, n * 32:(n + 1) * 32], pa[:, n * 32:(n + 1) * 32],
                                      gb[:, j * 16 + n: j * 16 + n + 1], None, OP.mult)
            return ins
        p.emit("vector", _ev, waits=[("tc", p.c("tc")), ("txp", p.c("txp"))], inc="vpe")
        pb = ps[2 + j % 2]

        def _mm2(t, j=j, pb=pb):
            ins = None
            for c in range(NCD):
                ins = t.matmul(pb[:, 0:32], xn0fm[:, c * TOK + j * 128: c * TOK + j * 128 + 128],
                               abuf2[:, c * 32:(c + 1) * 32], start=(c == 0), stop=(c == NCD - 1))
            return ins
        p.emit("tensor", _mm2, waits=[("ldw", p.c("ldw")), ("vpe2", p.c("vpe2"))], inc="tc")
        p.emit("vector", lambda v, pb=pb: v.tensor_copy(r3[:, 512:544], pb[:, 0:32]),
               waits=[("tc", p.c("tc")), ("txp", p.c("txp"))], inc="vpe2")
        emit_transpose_tile(
            p, lambda c: r3[:, c * 128:c * 128 + (128 if c < 4 else 32)],
            lambda c, j=j: (axfm[:, j * 640 + c * 128: j * 640 + c * 128 + 128] if c < 4
                            else axfm[0:32, j * 640 + 512: j * 640 + 640]),
            identb[:, :], (pst1, pst2), 5,
            extra_waits=[("vpe", p.c("vpe")), ("vpe2", p.c("vpe2"))],
            rows_fn=lambda c: 128 if c < 4 else 32)

    for ob in range(8):
        p.dma(g, wbuf[:, :].rearrange("p (c n) -> p c n", c=NCD),
              kern[:, ob * 512:(ob + 1) * 512].rearrange("(c p) n -> p c n", p=128),
              waits=[("tc", p.c("tc"))], inc="ldw")
        p.dma(g, brow2[0:1, :], bias0[0:1, ob * 512:(ob + 1) * 512],
              waits=[("tc", p.c("tc"))], inc="ldw")
        p.dma(g, r5[:, 0:2048].rearrange("p (c n) -> p c n", c=4),
              bcomb[0:512, ob * 512:(ob + 1) * 512].rearrange("(c p) n -> p c n", p=128),
              waits=[("tc", p.c("tc"))], inc="ldw")
        p.dma(g, bbuf2[:, :], bcomb[512:544, ob * 512:(ob + 1) * 512],
              waits=[("tc", p.c("tc"))], inc="ldw")
        for j in range(NT):
            pa = ps[j % 2]
            p.dma("sync", stage2[:, :], x0[j * 128:(j + 1) * 128, ob * 512:(ob + 1) * 512],
                  waits=[("vpe", p.c("vpe"))], inc="ldx")

            def _mm(t, j=j, pa=pa):
                t.matmul(pa[:, 0:512], ones1[0:1, 0:128], brow2[0:1, :], start=True, stop=False)
                for c in range(NCD):
                    t.matmul(pa[:, 0:512], xn0fm[:, c * TOK + j * 128: c * TOK + j * 128 + 128],
                             wbuf[:, c * 512:(c + 1) * 512], start=False, stop=False)
                for c in range(4):
                    t.matmul(pa[:, 0:512], axfm[:, j * 640 + c * 128: j * 640 + c * 128 + 128],
                             r5[:, c * 512:(c + 1) * 512], start=False, stop=False)
                return t.matmul(pa[:, 0:512], axfm[0:32, j * 640 + 512: j * 640 + 640],
                                bbuf2[:, :], start=False, stop=True)
            p.emit("tensor", _mm, waits=[("ldw", p.c("ldw")), ("tvx", p.c("tvx")),
                                         ("vpe", p.c("vpe"))], inc="tc")
            p.emit("vector", lambda v, pa=pa: v.scalar_tensor_tensor(
                stage[:, :], pa[:, 0:512], 1.0, stage2[:, :], OP.mult, OP.add),
                waits=[("tc", p.c("tc")), ("ldx", p.c("ldx")), ("stx", p.c("stx"))], inc="vpe")
            p.dma("sync", x1part[j * 128:(j + 1) * 128, ob * 512:(ob + 1) * 512], stage[:, :],
                  waits=[("vpe", p.c("vpe"))], inc="stx")

    for c_ in range(32):
        m_, u0_ = (c_ // 2) % 4, 256 * (c_ // 8) + (c_ % 2) * 128
        p.dma(g, v0[:, c_ * 1024:(c_ + 1) * 1024],
              ex_v0[m_ * 1024 + u0_: m_ * 1024 + u0_ + 128, :],
              waits=[("tc", p.c("tc")), ("cc", 2)], inc="ldA")
    p.dma(g, r4[:, 0:8192].rearrange("p (h t) -> p h t", h=8),
          q0fm_d.rearrange("(h p) t -> p h t", p=128),
          waits=[("tc", p.c("tc"))], inc="ldA")
    pt = r5
    attn0fm = r4
    recipb = r4f
    kbuf = r4

    for gi in range(NG):
        for kc in range(32):
            kb0 = 10752 + (kc % 2) * 1024
            m_, u0_ = (kc // 2) % 4, 256 * (kc // 8) + (kc % 2) * 128
            p.dma("sync", kbuf[:, kb0:kb0 + 1024].rearrange("p (c n) -> p c n", c=8),
                  X["ex_k0"][m_ * 1024: (m_ + 1) * 1024, u0_:u0_ + 128].rearrange(
                      "(h p) u -> p h u", p=128),
                  waits=[("tc", p.c("tc") - 1), ("cc", 2)], inc="ldk")
            pa = ps[kc % 2]

            def _mm(t, gi=gi, kc=kc, pa=pa, kb0=kb0):
                ins = None
                for c in range(8):
                    ins = t.matmul(pa[:, 0:256], kbuf[:, kb0 + c * 128: kb0 + c * 128 + 128],
                                   r4[:, c * TOK + gi * 256: c * TOK + gi * 256 + 256],
                                   start=(c == 0), stop=(c == 7))
                return ins
            p.emit("tensor", _mm, waits=[("ldk", p.c("ldk")), ("ldA", p.c("ldA")),
                                         ("scA", p.c("scA") - 1)], inc="tc")
            p.emit("scalar", lambda s, kc=kc, pa=pa: s.activation(
                pt[:, kc * 256:(kc + 1) * 256], pa[:, 0:256], AF.Exp,
                bias=mexpb[:, 0:1], scale=SC0),
                waits=[("tc", p.c("tc")), ("vc", 4)], inc="scA")
        def _sums(t):
            ins = None
            for kc in range(32):
                ins = t.matmul(psum1[0:1, 0:256], onescol[:, :], pt[:, kc * 256:(kc + 1) * 256],
                               start=(kc == 0), stop=(kc == 31))
            return ins
        p.emit("tensor", _sums, waits=[("scA", p.c("scA"))], inc="tc")
        p.emit("vector", lambda v: v.tensor_copy(rowsf[0:1, 0:256], psum1[0:1, 0:256]),
               waits=[("tc", p.c("tc"))], inc="vpe")
        p.emit("vector", lambda v: v.reciprocal(rowsf[0:1, 256:512], rowsf[0:1, 0:256]),
               waits=[("vpe", p.c("vpe"))], inc="vpe")
        p.emit("tensor", lambda t: t.matmul(psm[:, 0:256], onesf[0:1, 0:128],
                                            rowsf[0:1, 256:512], start=True, stop=True),
               waits=[("vpe", p.c("vpe"))], inc="tc")
        p.emit("vector", lambda v: v.tensor_copy(recipb[:, 5120:5376], psm[:, 0:256]),
               waits=[("tc", p.c("tc"))], inc="vpe")
        for vd in range(8):
            pb = ps[2 + vd % 2]

            def _pv(t, vd=vd, pb=pb):
                ins = None
                for kc in range(32):
                    ins = t.matmul(pb[:, 0:256], v0[:, kc * 1024 + vd * 128: kc * 1024 + vd * 128 + 128],
                                   pt[:, kc * 256:(kc + 1) * 256], start=(kc == 0), stop=(kc == 31))
                return ins
            p.emit("tensor", _pv, waits=[("vpe", p.c("vpe")), ("vpv", p.c("vpv"))], inc="tc")
            p.emit("vector", lambda v, vd=vd, pb=pb: v.tensor_tensor(
                attn0fm[:, 8192 + vd * 256: 8192 + (vd + 1) * 256], pb[:, 0:256],
                recipb[:, 5120:5376], OP.mult),
                waits=[("tc", p.c("tc"))], inc="vpv")
        for ob in range(8):
            p.dma(g, wbuf[:, 0:8 * 512].rearrange("p (c n) -> p c n", c=8),
                  wo[:, ob * 512:(ob + 1) * 512].rearrange("(c p) n -> p c n", p=128),
                  waits=[("tc", p.c("tc"))], inc="ldw")
            p.dma(g, brow2[0:1, :], bo[0:1, ob * 512:(ob + 1) * 512],
                  waits=[("tc", p.c("tc"))], inc="ldw")
            for j2 in range(2):
                j = 2 * gi + j2
                pa = ps[j2 % 2]
                p.dma("sync", stage2[:, :], x1part[j * 128:(j + 1) * 128, ob * 512:(ob + 1) * 512],
                      waits=[("vpe", p.c("vpe"))], inc="ldx")

                def _mm(t, j2=j2, pa=pa):
                    t.matmul(pa[:, 0:512], ones1[0:1, 0:128], brow2[0:1, :], start=True, stop=False)
                    ins = None
                    for c in range(8):
                        ins = t.matmul(pa[:, 0:512],
                                       attn0fm[:, 8192 + c * 256 + j2 * 128: 8192 + c * 256 + j2 * 128 + 128],
                                       wbuf[:, c * 512:(c + 1) * 512], start=False, stop=(c == 7))
                    return ins
                p.emit("tensor", _mm, waits=[("ldw", p.c("ldw")), ("vpv", p.c("vpv")),
                                             ("vpe", p.c("vpe"))], inc="tc")
                p.emit("vector", lambda v, pa=pa: v.scalar_tensor_tensor(
                    stage[:, :], pa[:, 0:512], 1.0, stage2[:, :], OP.mult, OP.add),
                    waits=[("tc", p.c("tc")), ("ldx", p.c("ldx")), ("stx", p.c("stx"))], inc="vpe")
                p.dma("sync", x1part[j * 128:(j + 1) * 128, ob * 512:(ob + 1) * 512], stage[:, :],
                      waits=[("vpe", p.c("vpe"))], inc="stx")
    L = dict(locals())
    return _finish_launch2(nc, es, p, L)


def _finish_launch2(nc, es, p, L):
    g = "gpsimd"
    arena = L["arena"]; wbuf = L["wbuf"]; r3 = L["r3"]; r4 = L["r4"]; r5 = L["r5"]
    r3f = L["r3f"]; r4f = L["r4f"]
    stage = L["stage"]; stage2 = L["stage2"]; stageb = L["stageb"]; stats = L["stats"]
    brow2 = L["brow2"]; ones1 = L["ones1"]; onesf = L["onesf"]; onescol = L["onescol"]
    identb = L["identb"]; cossb = L["cossb"]; sinsb = L["sinsb"]
    wqb = L["wqb"]; wkvb = L["wkvb"]; bthrb = L["bthrb"]; combo = L["combo"]
    gatew = L["gatew"]; hgfm = L["hgfm"]; bcatb = L["bcatb"]
    hgbuf = L["hgbuf"]; rtmp = L["rtmp"]; rtmp2 = L["rtmp2"]
    ps = L["ps"]; pst1 = L["pst1"]; pst2 = L["pst2"]; psum1 = L["psum1"]; psm = L["psm"]
    x1part = L["x1part"]; x0 = L["x0"]
    xn1fm = arena
    qa = L["qa"]; qbn = L["qbn"]; qbp = L["qbp"]; kva = L["kva"]
    kvbk = L["kvbk"]; kvbv = L["kvbv"]; combow = L["combow"]
    wbaset = L["wbaset"]; bbase = L["bbase"]; acat = L["acat"]; bcat_d = L["bcat_d"]
    q1nfm_d = L["q1nfm_d"]; q1pe_d = L["q1pe_d"]; k1nfm_d = L["k1nfm_d"]
    k1pe_d = L["k1pe_d"]; v1tm_d = L["v1tm_d"]; ikfm_d = L["ikfm_d"]
    iqfm_d = L["iqfm_d"]; iw_d = L["iw_d"]; x2p_d = L["x2p_d"]
    idx_wq = None

    x1tile = r3f
    xn1tile = r3
    junk = r3
    for j in range(NT):
        p.dma(g, x1tile[:, 0:D], x1part[j * 128:(j + 1) * 128, :],
              waits=[("sc", p.c("sc")), ("stx", p.c("stx")), ("tvx", p.c("tvx"))], inc="ldx2")
        p.emit("vector", lambda v: v.memset(stats[:, 15:16], 0.0),
               waits=[("ldx2", p.c("ldx2"))], inc="vc")
        emit_ln(p, x1tile[:, 0:D], xn1tile[:, 8192:8192 + D], stats, junk[:, 12288:12288 + D])
        emit_transpose_tile(
            p, lambda c: xn1tile[:, 8192 + c * 128: 8192 + (c + 1) * 128],
            lambda c, j=j: xn1fm[:, c * TOK + j * 128: c * TOK + j * 128 + 128],
            identb[:, :], (pst1, pst2), NCD,
            extra_waits=[("sc", p.c("sc")), ("tc", p.c("tc"))])

    def tm_proj(wsrc, col0, ncols, evac_fn, nchunks=NCD, lhs_fm=None, lhs_tok=TOK,
                extra_w=()):
        nq = 4 if ncols < 256 else 1
        cq = nchunks // nq
        for q_ in range(nq):
            p.dma(g, wbuf[:, q_ * cq * ncols:(q_ + 1) * cq * ncols].rearrange(
                "p (c n) -> p c n", c=cq),
                wsrc[q_ * cq * 128:(q_ + 1) * cq * 128, col0:col0 + ncols].rearrange(
                    "(c p) n -> p c n", p=128),
                waits=[("tc", p.c("tc"))], inc="ldw")
        src = lhs_fm if lhs_fm is not None else xn1fm
        for j in range(NT):
            pa = ps[j % 2]

            def _mm(t, j=j, pa=pa):
                ins = None
                for c in range(nchunks):
                    ins = t.matmul(pa[:, 0:ncols],
                                   src[:, c * lhs_tok + j * 128: c * lhs_tok + j * 128 + 128],
                                   wbuf[:, c * ncols:(c + 1) * ncols],
                                   start=(c == 0), stop=(c == nchunks - 1))
                return ins
            p.emit("tensor", _mm, waits=[("ldw", p.c("ldw")), ("tvx", p.c("tvx")),
                                         ("vpe", p.c("vpe"))] + list(extra_w), inc="tc")
            evac_fn(j, pa)

    def combo_evac(j, pa):
        p.emit("vector", lambda v, j=j, pa=pa: v.tensor_copy(
            combo[:, j * 32: j * 32 + 32], pa[:, 0:32]), waits=[("tc", p.c("tc"))], inc="vpe")
    tm_proj(combow, 0, 32, combo_evac)
    for j in range(NT):
        p.emit("vector", lambda v, j=j: v.tensor_scalar(
            stage[:, 0:16], combo[:, j * 32 + 9: j * 32 + 25], 0.25, None, OP.mult),
            waits=[("vpe", p.c("vpe")), ("stx", p.c("stx"))], inc="vpe")
        p.dma("sync", iw_d[j * 128:(j + 1) * 128, :], stage[:, 0:16],
              waits=[("vpe", p.c("vpe"))], inc="stx")

    qlat = r4
    for blk in range(3):
        def qlat_evac(j, pa, blk=blk):
            p.emit("vector", lambda v, j=j, pa=pa, blk=blk: v.tensor_copy(
                qlat[:, j * 1536 + blk * 512: j * 1536 + (blk + 1) * 512], pa[:, 0:512]),
                waits=[("tc", p.c("tc"))], inc="vpe")
        tm_proj(qa, blk * 512, 512, qlat_evac)
    def emit_rms_inplace(buf_ap, dim, wb, jtag):
        p.emit("scalar", lambda s: s.activation(
            junk[:, 12288:12288 + dim], buf_ap, AF.Square, accum_out=stats[:, 0:1]),
            waits=[("vpe", p.c("vpe")), ("vc", p.c("vc"))], inc="sc")
        p.emit("vector", lambda v: v.tensor_scalar(
            stats[:, 1:2], stats[:, 0:1], 1.0 / dim, 1e-5, OP.mult, OP.add),
            waits=[("sc", p.c("sc"))], inc="vc")
        p.emit("scalar", lambda s: s.activation(stats[:, 2:3], stats[:, 1:2], AF.Ln),
               waits=[("vc", p.c("vc"))], inc="sc")
        p.emit("scalar", lambda s: s.activation(stats[:, 1:2], stats[:, 2:3], AF.Exp,
                                                scale=-0.5), inc="sc")
        p.emit("vector", lambda v: v.tensor_scalar(buf_ap, buf_ap, stats[:, 1:2], None,
                                                   OP.mult),
               waits=[("sc", p.c("sc"))], inc="vc")
        p.emit("vector", lambda v: v.tensor_tensor(buf_ap, buf_ap, wb, OP.mult),
               waits=[("vc", p.c("vc"))], inc="vpe")
    for j in range(NT):
        emit_rms_inplace(qlat[:, j * 1536:(j + 1) * 1536], 1536, wqb[:, :], j)
    if L["dbg_qlat"] is not None:
        p.dma("sync", L["dbg_qlat"][:, :], qlat[:, 0:8 * 1536], waits=[("vpe", p.c("vpe"))], inc="st")
        p.dma("sync", L["dbg_combo"][:, :], combo[:, :], waits=[("vpe", p.c("vpe"))], inc="st")

    latfm = r5
    for st in range(2):
        base_tvx = p.c("tvx")
        for jj in range(4):
            j = st * 4 + jj
            emit_transpose_tile(
                p, lambda c, j=j: qlat[:, j * 1536 + c * 128: j * 1536 + (c + 1) * 128],
                lambda c, jj=jj: latfm[:, c * 512 + jj * 128: c * 512 + jj * 128 + 128],
                identb[:, :], (pst1, pst2), 12,
                extra_waits=[("vpe", p.c("vpe")), ("tc", p.c("tc"))])
        if L["dbg_latfm"] is not None and st == 0:
            p.dma("sync", L["dbg_latfm"][:, :], latfm[:, 0:6144],
                  waits=[("tvx", p.c("tvx"))], inc="st")
        for wblk in range(8):
            p.dma(g, wbuf[:, 0:12 * 512].rearrange("p (c n) -> p c n", c=12),
                  qbn[:, wblk * 512:(wblk + 1) * 512].rearrange("(c p) n -> p c n", p=128),
                  waits=[("tc", p.c("tc"))], inc="ldw")
            for fsub in range(4):
                pa = ps[fsub % 2]

                def _mm(t, fsub=fsub, pa=pa):
                    ins = None
                    for c in range(12):
                        ins = t.matmul(pa[:, 0:512],
                                       wbuf[:, c * 512 + fsub * 128: c * 512 + fsub * 128 + 128],
                                       latfm[:, c * 512:(c + 1) * 512],
                                       start=(c == 0), stop=(c == 11))
                    return ins
                p.emit("tensor", _mm, waits=[("ldw", p.c("ldw")), ("tvx", p.c("tvx")),
                                             ("vpe", p.c("vpe"))], inc="tc")
                p.emit("vector", lambda v, pa=pa: v.tensor_copy(stageb[:, :], pa[:, 0:512]),
                       waits=[("tc", p.c("tc")), ("stx", p.c("stx"))], inc="vpe")
                p.dma("sync", q1nfm_d[wblk * 512 + fsub * 128: wblk * 512 + (fsub + 1) * 128,
                                      st * 512:(st + 1) * 512], stageb[:, :],
                      waits=[("vpe", p.c("vpe"))], inc="stx")
        for wblk in range(4):
            p.dma(g, wbuf[:, 0:12 * 512].rearrange("p (c n) -> p c n", c=12),
                  qbp[:, wblk * 512:(wblk + 1) * 512].rearrange("(c p) n -> p c n", p=128),
                  waits=[("tc", p.c("tc"))], inc="ldw")
            for jj in range(4):
                j = st * 4 + jj
                pa = ps[jj % 2]

                def _mm(t, jj=jj, pa=pa):
                    ins = None
                    for c in range(12):
                        ins = t.matmul(pa[:, 0:512],
                                       latfm[:, c * 512 + jj * 128: c * 512 + jj * 128 + 128],
                                       wbuf[:, c * 512:(c + 1) * 512],
                                       start=(c == 0), stop=(c == 11))
                    return ins
                p.emit("tensor", _mm, waits=[("ldw", p.c("ldw")), ("tvx", p.c("tvx")),
                                             ("vpe", p.c("vpe"))], inc="tc")
                emit_rope_block(p, pa, j, 8, cossb, sinsb, rtmp, rtmp2, stageb)
                p.dma("sync", q1pe_d[j * 128:(j + 1) * 128, wblk * 512:(wblk + 1) * 512],
                      stageb[:, :], waits=[("vpe", p.c("vpe"))], inc="stx")
    return _finish_launch2b(nc, es, p, L)


def _finish_launch2b(nc, es, p, L):
    g = "gpsimd"
    arena = L["arena"]; wbuf = L["wbuf"]; r3 = L["r3"]; r4 = L["r4"]; r5 = L["r5"]
    r3f = L["r3f"]
    stage = L["stage"]; stage2 = L["stage2"]; stageb = L["stageb"]; stats = L["stats"]
    brow2 = L["brow2"]; ones1 = L["ones1"]; onescol = L["onescol"]
    identb = L["identb"]; cossb = L["cossb"]; sinsb = L["sinsb"]
    wkvb = L["wkvb"]; bthrb = L["bthrb"]; combo = L["combo"]
    gatew = L["gatew"]; hgfm = L["hgfm"]; bcatb = L["bcatb"]
    hgbuf = L["hgbuf"]; rtmp = L["rtmp"]; rtmp2 = L["rtmp2"]
    ps = L["ps"]; pst1 = L["pst1"]; pst2 = L["pst2"]; psm = L["psm"]
    x1part = L["x1part"]
    kva = L["kva"]; kvbk = L["kvbk"]; kvbv = L["kvbv"]
    wbaset = L["wbaset"]; bbase = L["bbase"]; acat = L["acat"]; bcat_d = L["bcat_d"]
    k1nfm_d = L["k1nfm_d"]; k1pe_d = L["k1pe_d"]; v1tm_d = L["v1tm_d"]
    ikfm_d = L["ikfm_d"]; iqfm_d = L["iqfm_d"]; x2p_d = L["x2p_d"]
    idx_wq_d = L["idx_wq_d"]; idx_wk_d = L["idx_wk_d"]
    xn1fm = arena
    junk = r3

    ckv = r5
    latkvfm = r5
    for blk, (c0, nc_) in enumerate([(0, 512), (512, 64)]):
        p.dma(g, wbuf[:, 0:NCD * nc_].rearrange("p (c n) -> p c n", c=NCD),
              kva[:, c0:c0 + nc_].rearrange("(c p) n -> p c n", p=128),
              waits=[("tc", p.c("tc"))], inc="ldw")
        for j in range(NT):
            pa = ps[j % 2]

            def _mm(t, j=j, pa=pa, nc_=nc_):
                ins = None
                for c in range(NCD):
                    ins = t.matmul(pa[:, 0:nc_],
                                   xn1fm[:, c * TOK + j * 128: c * TOK + j * 128 + 128],
                                   wbuf[:, c * nc_:(c + 1) * nc_],
                                   start=(c == 0), stop=(c == NCD - 1))
                return ins
            p.emit("tensor", _mm, waits=[("ldw", p.c("ldw")), ("vpe", p.c("vpe"))], inc="tc")
            p.emit("vector", lambda v, j=j, pa=pa, c0=c0, nc_=nc_: v.tensor_copy(
                ckv[:, j * 576 + c0: j * 576 + c0 + nc_], pa[:, 0:nc_]),
                waits=[("tc", p.c("tc"))], inc="vpe")
    for j in range(NT):
        buf = ckv[:, j * 576: j * 576 + 512]
        p.emit("scalar", lambda s, buf=buf: s.activation(
            junk[:, 0:512], buf, AF.Square, accum_out=stats[:, 0:1]),
            waits=[("vpe", p.c("vpe")), ("vc", p.c("vc"))], inc="sc")
        p.emit("vector", lambda v: v.tensor_scalar(
            stats[:, 1:2], stats[:, 0:1], 1.0 / 512, 1e-5, OP.mult, OP.add),
            waits=[("sc", p.c("sc"))], inc="vc")
        p.emit("scalar", lambda s: s.activation(stats[:, 2:3], stats[:, 1:2], AF.Ln),
               waits=[("vc", p.c("vc"))], inc="sc")
        p.emit("scalar", lambda s: s.activation(stats[:, 1:2], stats[:, 2:3], AF.Exp,
                                                scale=-0.5), inc="sc")
        p.emit("vector", lambda v, buf=buf: v.tensor_scalar(buf, buf, stats[:, 1:2], None,
                                                            OP.mult),
               waits=[("sc", p.c("sc"))], inc="vc")
        p.emit("vector", lambda v, buf=buf: v.tensor_tensor(buf, buf, wkvb[:, :], OP.mult),
               waits=[("vc", p.c("vc"))], inc="vpe")
        pe = ckv[:, j * 576 + 512: j * 576 + 576]
        cosv = cossb[:, j * 32:(j + 1) * 32].unsqueeze(1).to_broadcast([128, 1, 32])
        sinv = sinsb[:, j * 32:(j + 1) * 32].unsqueeze(1).to_broadcast([128, 1, 32])
        src = pe.rearrange("p (h r two) -> p h r two", h=1, two=2)
        dst = stageb[:, 0:64].rearrange("p (h r two) -> p h r two", h=1, two=2)
        t1 = rtmp[:, 0:32].rearrange("p (h r) -> p h r", h=1)
        t2 = rtmp2[:, 0:32].rearrange("p (h r) -> p h r", h=1)

        def _r(v, src=src, dst=dst, t1=t1, t2=t2, cosv=cosv, sinv=sinv):
            v.tensor_tensor(t1, src[:, :, :, 0], cosv, OP.mult)
            v.tensor_tensor(t2, src[:, :, :, 1], sinv, OP.mult)
            v.tensor_tensor(dst[:, :, :, 0], t1, t2, OP.subtract)
            v.tensor_tensor(t1, src[:, :, :, 1], cosv, OP.mult)
            v.tensor_tensor(t2, src[:, :, :, 0], sinv, OP.mult)
            return v.tensor_tensor(dst[:, :, :, 1], t1, t2, OP.add)
        p.emit("vector", _r, waits=[("stx", p.c("stx"))], inc="vpe")
        p.dma("sync", k1pe_d[j * 128:(j + 1) * 128, :], stageb[:, 0:64],
              waits=[("vpe", p.c("vpe"))], inc="stx")
    for st in range(2):
        for jj in range(4):
            j = st * 4 + jj
            emit_transpose_tile(
                p, lambda c, j=j: ckv[:, j * 576 + c * 128: j * 576 + (c + 1) * 128],
                lambda c, jj=jj: latkvfm[:, 4608 + c * 512 + jj * 128: 4608 + c * 512 + jj * 128 + 128],
                identb[:, :], (pst1, pst2), 4,
                extra_waits=[("vpe", p.c("vpe")), ("tc", p.c("tc"))])
        for wblk in range(8):
            p.dma(g, wbuf[:, 0:4 * 512].rearrange("p (c n) -> p c n", c=4),
                  kvbk[:, wblk * 512:(wblk + 1) * 512].rearrange("(c p) n -> p c n", p=128),
                  waits=[("tc", p.c("tc"))], inc="ldw")
            for fsub in range(4):
                pa = ps[fsub % 2]

                def _mm(t, fsub=fsub, pa=pa):
                    ins = None
                    for c in range(4):
                        ins = t.matmul(pa[:, 0:512],
                                       wbuf[:, c * 512 + fsub * 128: c * 512 + fsub * 128 + 128],
                                       latkvfm[:, 4608 + c * 512: 4608 + (c + 1) * 512],
                                       start=(c == 0), stop=(c == 3))
                    return ins
                p.emit("tensor", _mm, waits=[("ldw", p.c("ldw")), ("tvx", p.c("tvx")),
                                             ("vpe", p.c("vpe"))], inc="tc")
                p.emit("vector", lambda v, pa=pa: v.tensor_copy(stageb[:, :], pa[:, 0:512]),
                       waits=[("tc", p.c("tc")), ("stx", p.c("stx"))], inc="vpe")
                p.dma("sync", k1nfm_d[wblk * 512 + fsub * 128: wblk * 512 + (fsub + 1) * 128,
                                      st * 512:(st + 1) * 512], stageb[:, :],
                      waits=[("vpe", p.c("vpe"))], inc="stx")
        for wblk in range(8):
            p.dma(g, wbuf[:, 0:4 * 512].rearrange("p (c n) -> p c n", c=4),
                  kvbv[:, wblk * 512:(wblk + 1) * 512].rearrange("(c p) n -> p c n", p=128),
                  waits=[("tc", p.c("tc"))], inc="ldw")
            for jj in range(4):
                j = st * 4 + jj
                pa = ps[jj % 2]

                def _mm(t, jj=jj, pa=pa):
                    ins = None
                    for c in range(4):
                        ins = t.matmul(pa[:, 0:512],
                                       latkvfm[:, 4608 + c * 512 + jj * 128: 4608 + c * 512 + jj * 128 + 128],
                                       wbuf[:, c * 512:(c + 1) * 512],
                                       start=(c == 0), stop=(c == 3))
                    return ins
                p.emit("tensor", _mm, waits=[("ldw", p.c("ldw")), ("tvx", p.c("tvx")),
                                             ("vpe", p.c("vpe"))], inc="tc")
                p.emit("vector", lambda v, pa=pa: v.tensor_copy(stageb[:, :], pa[:, 0:512]),
                       waits=[("tc", p.c("tc")), ("stx", p.c("stx"))], inc="vpe")
                p.dma("sync", v1tm_d[j * 128:(j + 1) * 128, wblk * 512:(wblk + 1) * 512],
                      stageb[:, :], waits=[("vpe", p.c("vpe"))], inc="stx")

    for wblk in range(4):
        p.dma(g, wbuf[:, 0:NCD * 512].rearrange("p (c n) -> p c n", c=NCD),
              idx_wq_d[:, wblk * 512:(wblk + 1) * 512].rearrange("(c p) n -> p c n", p=128),
              waits=[("tc", p.c("tc"))], inc="ldw")
        for fsub in range(4):
            for st in range(2):
                pa = ps[(fsub * 2 + st) % 2]

                def _mm(t, fsub=fsub, st=st, pa=pa):
                    ins = None
                    for c in range(NCD):
                        ins = t.matmul(pa[:, 0:512],
                                       wbuf[:, c * 512 + fsub * 128: c * 512 + fsub * 128 + 128],
                                       xn1fm[:, c * TOK + st * 512: c * TOK + st * 512 + 512],
                                       start=(c == 0), stop=(c == NCD - 1))
                    return ins
                p.emit("tensor", _mm, waits=[("ldw", p.c("ldw")), ("vpe", p.c("vpe"))], inc="tc")
                p.emit("vector", lambda v, pa=pa: v.tensor_copy(stageb[:, :], pa[:, 0:512]),
                       waits=[("tc", p.c("tc")), ("stx", p.c("stx"))], inc="vpe")
                p.dma("sync", iqfm_d[wblk * 512 + fsub * 128: wblk * 512 + (fsub + 1) * 128,
                                     st * 512:(st + 1) * 512], stageb[:, :],
                      waits=[("vpe", p.c("vpe"))], inc="stx")
    p.dma(g, wbuf[:, 0:NCD * 128].rearrange("p (c n) -> p c n", c=NCD),
          idx_wk_d.rearrange("(c p) n -> p c n", p=128),
          waits=[("tc", p.c("tc"))], inc="ldw")
    for st in range(2):
        pa = ps[st % 2]

        def _mm(t, st=st, pa=pa):
            ins = None
            for c in range(NCD):
                ins = t.matmul(pa[:, 0:512], wbuf[:, c * 128:(c + 1) * 128],
                               xn1fm[:, c * TOK + st * 512: c * TOK + st * 512 + 512],
                               start=(c == 0), stop=(c == NCD - 1))
            return ins
        p.emit("tensor", _mm, waits=[("ldw", p.c("ldw")), ("vpe", p.c("vpe"))], inc="tc")
        p.emit("vector", lambda v, pa=pa: v.tensor_copy(stageb[:, :], pa[:, 0:512]),
               waits=[("tc", p.c("tc")), ("stx", p.c("stx"))], inc="vpe")
        p.dma("sync", ikfm_d[:, st * 512:(st + 1) * 512], stageb[:, :],
              waits=[("vpe", p.c("vpe"))], inc="stx")

    acatb = wbuf
    p.dma(g, acatb[:, 0:NCD * 128].rearrange("p (c n) -> p c n", c=NCD),
          acat.rearrange("(c p) n -> p c n", p=128), waits=[("tc", p.c("tc"))], inc="ldw")
    p.dma(g, bcatb[:, :], bcat_d, waits=[("tc", p.c("tc"))], inc="ldw")
    for j in range(NT):
        cg = combo[:, j * 32: j * 32 + 8]
        p.emit("vector", lambda v, cg=cg: v.reduce_max(stats[:, 0:1], cg, AX, negate=True),
               waits=[("vpe", p.c("vpe"))], inc="vc")
        p.emit("scalar", lambda s, cg=cg: s.activation(gatew[:, 0:8], cg, AF.Exp,
                                                       bias=stats[:, 0:1], scale=1.0,
                                                       accum_out=stats[:, 1:2]),
               waits=[("vc", p.c("vc"))], inc="sc")
        p.emit("vector", lambda v: v.reciprocal(stats[:, 2:3], stats[:, 1:2]),
               waits=[("sc", p.c("sc"))], inc="vc")
        p.emit("vector", lambda v: v.tensor_scalar(gatew[:, 0:8], gatew[:, 0:8],
                                                   stats[:, 2:3], None, OP.mult),
               waits=[("vc", p.c("vc"))], inc="vc")
        p.emit("vector", lambda v, j=j: v.tensor_tensor(
            stats[:, 3:4], combo[:, j * 32 + 8: j * 32 + 9], bthrb[:, :], OP.add),
               waits=[("vc", p.c("vc"))], inc="vc")
        p.emit("scalar", lambda s: s.activation(stats[:, 4:5], stats[:, 3:4], AF.Exp,
                                                scale=-1.0),
               waits=[("vc", p.c("vc"))], inc="sc")
        p.emit("vector", lambda v: v.tensor_scalar(stats[:, 4:5], stats[:, 4:5], 1.0,
                                                   None, OP.add),
               waits=[("sc", p.c("sc")), ("vc", p.c("vc"))], inc="vc")
        p.emit("vector", lambda v: v.reciprocal(stats[:, 5:6], stats[:, 4:5]),
               waits=[("vc", p.c("vc"))], inc="vc")
        p.emit("vector", lambda v: v.tensor_scalar(stats[:, 6:7], stats[:, 5:6], 0.125,
                                                   None, OP.mult),
               waits=[("vc", p.c("vc"))], inc="vc")
        p.emit("vector", lambda v: v.tensor_scalar(gatew[:, 8:16], gatew[:, 0:8],
                                                   stats[:, 6:7], 0.0, OP.subtract, OP.max),
               waits=[("vc", p.c("vc"))], inc="vc")
        p.emit("vector", lambda v: v.reduce_sum(stats[:, 7:8], gatew[:, 8:16], AX),
               waits=[("vc", p.c("vc"))], inc="vc")
        p.emit("vector", lambda v: v.tensor_scalar(stats[:, 8:9], stats[:, 7:8], 0.0, None,
                                                   OP.is_equal),
               waits=[("vc", p.c("vc"))], inc="vc")
        p.emit("vector", lambda v: v.tensor_tensor(stats[:, 7:8], stats[:, 7:8],
                                                   stats[:, 8:9], OP.add),
               waits=[("vc", p.c("vc"))], inc="vc")
        p.emit("vector", lambda v: v.reciprocal(stats[:, 9:10], stats[:, 7:8]),
               waits=[("vc", p.c("vc"))], inc="vc")
        p.emit("vector", lambda v: v.tensor_scalar(gatew[:, 8:16], gatew[:, 8:16],
                                                   stats[:, 9:10], 2.0, OP.mult, OP.mult),
               waits=[("vc", p.c("vc"))], inc="vc")
        pa = ps[j % 2]

        def _mm(t, j=j, pa=pa):
            ins = None
            for c in range(NCD):
                ins = t.matmul(pa[:, 0:128], xn1fm[:, c * TOK + j * 128: c * TOK + j * 128 + 128],
                               acatb[:, c * 128:(c + 1) * 128], start=(c == 0), stop=(c == NCD - 1))
            return ins
        p.emit("tensor", _mm, waits=[("ldw", p.c("ldw")), ("vpe", p.c("vpe"))], inc="tc")
        p.emit("vector", lambda v, pa=pa: v.tensor_copy(stageb[:, 0:128], pa[:, 0:128]),
               waits=[("tc", p.c("tc")), ("vc", p.c("vc"))], inc="vc")

        def _hg(v):
            ins = None
            for e in range(8):
                ins = v.tensor_scalar(hgbuf[:, e * 16:(e + 1) * 16], stageb[:, e * 16:(e + 1) * 16],
                                      gatew[:, 8 + e: 9 + e], None, OP.mult)
            return ins
        if L["dbg_wn"] is not None:
            p.dma("sync", L["dbg_st"][j], stats[:, :], waits=[("vc", p.c("vc"))], inc="st")
            p.emit("vector", lambda v, j=j: v.tensor_copy(
                L["dbg_wn"].tensor.ap()[:, j * 16:(j + 1) * 16], gatew[:, 0:16]) if False else v.tensor_copy(stage[:, 0:16], gatew[:, 0:16]), inc="vc")
            p.dma("sync", L["dbg_wn"][:, j * 16:(j + 1) * 16], stage[:, 0:16],
                  waits=[("vc", p.c("vc"))], inc="st")
            p.dma("sync", L["dbg_h"][:, j * 128:(j + 1) * 128], stageb[:, 0:128],
                  waits=[("vc", p.c("vc"))], inc="st")
        p.emit("vector", _hg, waits=[("txp", p.c("txp")), ("vc", p.c("vc"))], inc="vc")
        emit_transpose_tile(p, lambda c: hgbuf[:, 0:128],
                            lambda c, j=j: hgfm[:, j * 128:(j + 1) * 128],
                            identb[:, :], (pst1, pst2), 1,
                            extra_waits=[("vc", p.c("vc"))])
    if L["dbg_hgfm"] is not None:
        p.dma("sync", L["dbg_hgfm"][:, :], hgfm[:, :], waits=[("tvx", p.c("tvx"))], inc="st")
    X = L["X"]
    k1pe_d = L["k1pe_d"]
    kpeT_d = X["kpeT_d"]
    for j in range(NT):
        p.dma(g, wbuf[:, 2048:2048 + 64], k1pe_d[j * 128:(j + 1) * 128, :],
              waits=[("txq", p.c("txq")), ("stx", p.c("stx")), ("tc", p.c("tc"))],
              inc="ldq")
        bank = (pst1, pst2)[j % 2]
        half = (j % 2) * 128
        stq0 = p.c("stq")
        p.emit("tensor", lambda t, bank=bank: t.transpose(
            bank[0:64, 0:128], wbuf[:, 2048:2048 + 64], identb[:, :]),
            waits=[("ldq", p.c("ldq")), ("txp", p.c("txp")),
                   ("tvq", max(0, p.c("tvq") - 1)), ("tvx", p.c("tvx"))], inc="txq")
        p.emit("vector", lambda v, half=half, bank=bank: v.tensor_copy(
            stageb[0:64, half:half + 128], bank[0:64, 0:128]),
            waits=[("txq", p.c("txq")), ("stq", max(0, stq0 - 16)),
                   ("stx", p.c("stx"))], inc="tvq")
        p.dma("sync", kpeT_d[:, j * 128:(j + 1) * 128], stageb[0:64, half:half + 128],
              waits=[("tvq", p.c("tvq"))], inc="stq")
    ints = X["ints"]
    ccw = [("stx", p.c("stx")), ("stq", p.c("stq"))]
    emit_cc(p, ints["ikfm_d"], ints["ex_ik"], waits=ccw, sem="cci")
    emit_cc(p, ints["k1nfm_d"], ints["ex_k1"], waits=ccw, sem="cck")
    emit_cc(p, ints["kpeT_d"], ints["ex_kpe"], waits=ccw, sem="ccp")
    emit_cc(p, ints["v1tm_d"], ints["ex_v1"], waits=ccw, sem="ccv")

    for ob in range(8):
        p.dma(g, wbuf[:, :].rearrange("p (c n) -> p c n", c=NCD),
              wbaset[:, ob * 512:(ob + 1) * 512].rearrange("(c p) n -> p c n", p=128),
              waits=[("tc", p.c("tc")), ("txq", p.c("txq"))], inc="ldw")
        p.dma(g, brow2[0:1, :], bbase[0:1, ob * 512:(ob + 1) * 512],
              waits=[("tc", p.c("tc"))], inc="ldw")
        for j in range(NT):
            pa = ps[j % 2]
            p.dma("sync", stage2[:, :], x1part[j * 128:(j + 1) * 128, ob * 512:(ob + 1) * 512],
                  waits=[("vpe", p.c("vpe"))], inc="ldx")

            def _mm(t, j=j, pa=pa, ob=ob):
                t.matmul(pa[:, 0:512], ones1[0:1, 0:128], brow2[0:1, :], start=True, stop=False)
                for c in range(NCD):
                    t.matmul(pa[:, 0:512], xn1fm[:, c * TOK + j * 128: c * TOK + j * 128 + 128],
                             wbuf[:, c * 512:(c + 1) * 512], start=False, stop=False)
                return t.matmul(pa[:, 0:512], hgfm[:, j * 128:(j + 1) * 128],
                                bcatb[:, ob * 512:(ob + 1) * 512], start=False, stop=True)
            p.emit("tensor", _mm, waits=[("ldw", p.c("ldw")), ("tvx", p.c("tvx")),
                                         ("vpe", p.c("vpe"))], inc="tc")
            p.emit("vector", lambda v, pa=pa: v.scalar_tensor_tensor(
                stage[:, :], pa[:, 0:512], 1.0, stage2[:, :], OP.mult, OP.add),
                waits=[("tc", p.c("tc")), ("ldx", p.c("ldx")), ("stx", p.c("stx"))], inc="vpe")
            p.dma("sync", x2p_d[j * 128:(j + 1) * 128, ob * 512:(ob + 1) * 512], stage[:, :],
                  waits=[("vpe", p.c("vpe"))], inc="stx")

    X = L["X"]
    q1pe_d = L["q1pe_d"]; k1pe_d = L["k1pe_d"]
    q1pT_d = X["q1pT_d"]; kpeT_d = X["kpeT_d"]
    for j in range(NT):
        p.dma(g, wbuf[:, 0:2048], q1pe_d[j * 128:(j + 1) * 128, :],
              waits=[("stx", p.c("stx")), ("txq", p.c("txq"))], inc="ldq")
        for c in range(16):
            bank = (pst1, pst2)[c % 2]
            half = (c % 2) * 128
            stq0 = p.c("stq")
            p.emit("tensor", lambda t, c=c, bank=bank: t.transpose(
                bank[0:128, 0:128], wbuf[:, c * 128:(c + 1) * 128], identb[:, :]),
                waits=[("ldq", p.c("ldq")), ("txp", p.c("txp")),
                       ("tvq", max(0, p.c("tvq") - 1)), ("tvx", p.c("tvx"))], inc="txq")
            p.emit("vector", lambda v, half=half, bank=bank: v.tensor_copy(
                stageb[:, half:half + 128], bank[0:128, 0:128]),
                waits=[("txq", p.c("txq")), ("stq", max(0, stq0 - 16)),
                       ("stx", p.c("stx"))], inc="tvq")
            p.dma("sync", q1pT_d[c * 128:(c + 1) * 128, j * 128:(j + 1) * 128],
                  stageb[:, half:half + 128], waits=[("tvq", p.c("tvq"))], inc="stq")


SC1 = float(192.0 ** -0.5)
NBIS = 24


def emit_phase3(nc, es, p, A, X):
    iw_in = X["iw_d"]; cmasks = X["cmasks"]; x2part = X["x2p_d"]
    ow_d = X["ow"]; identb_d = X["identb_d"]; out_d = X["out_d"]
    q1nfm_d = X["q1nfm_d"]; q1pT_d = X["q1pT_d"]; iqfm_d = X["iqfm_d"]
    ex_k1 = X["ex_k1"]; ex_v1 = X["ex_v1"]; ex_kpe = X["ex_kpe"]; ex_ik = X["ex_ik"]
    dbg_bis = dbg_keep = dbg_sums = dbg_attn = None

    q1n = A.sb("q1n", [128, 32 * 256], BF16)
    q1p = A.sb("q1p", [64, 32 * 256], BF16)
    iqg = A.sb("iqg", [128, 16 * 256], BF16)
    iwg = A.sb("iwg", [128, 2 * 16], F32)
    ik_sb = A.sb("ik_sb", [128, 4096], BF16)
    kpe_sb = A.sb("kpe_sb", [64, 4096], BF16)
    acc = A.sb("acc", [128, 4096], F32)
    cmask_sb = A.sb("cmask_sb", [128, 4096], BF16)
    keep = A.sb("keep", [128, 2 * 4096], BF16)
    keepT = A.sb("keepT", [128, 32 * 256], BF16)
    pt = A.sb("pt_b", [128, 32 * 256], BF16)
    kb_sb = A.sb("kb_sb", [128, 4096], BF16)
    vb_sb = A.sb("vb_sb", [128, 4096], BF16)
    attnfm = A.sb("attnfm", [128, 32 * 256], BF16)
    relub = A.sb("relub", [128, 512], BF16)
    stage = A.sb("stage", [128, 512], F32)
    stage2 = A.sb("stage2", [128, 512], F32)
    stats = A.sb("stats", [128, 16], F32)
    bis = A.sb("bis", [128, 8], F32)
    maski = es.enter_context(nc.sbuf_tensor("maski", [128, 1], mybir.dt.int32))
    rowsf = A.sb("rowsf", [1, 512], F32)
    onesf = A.sb("onesf", [1, 128], F32)
    onescol = A.sb("onescol", [128, 1], BF16)
    ones1 = A.sb("ones1", [1, 512], BF16)
    identb = A.sb("identb_sb", [128, 128], BF16)
    recipb = A.sb("recipb", [128, 256], F32)
    mexpb = A.sb("mexpb3", [128, 1], F32)
    negt = A.sb("negt", [128, 4096], F32)
    wbuf = A.sb("wbuf", [128, 8192], BF16)

    ps = X["ps"]; pst1 = X["pst1"]; pst2 = X["pst2"]
    psum1 = X["psum1"]; psm = X["psm"]
    g = "gpsimd"

    p.dma(g, identb[:, :], identb_d)
    for m_ in range(4):
        for j_ in range(4):
            g0 = 256 * (m_ + 4 * j_)
            p.dma(g, ik_sb[:, g0:g0 + 256],
                  ex_ik[m_ * 128:(m_ + 1) * 128, 256 * j_:256 * j_ + 256],
                  waits=[("cci", 1)], inc="ld")
            p.dma(g, kpe_sb[:, g0:g0 + 256],
                  ex_kpe[m_ * 64:(m_ + 1) * 64, 256 * j_:256 * j_ + 256],
                  waits=[("ccp", 1)], inc="ld")
    p.emit("vector", lambda v: v.memset(ones1[:, :], 1.0), inc="vc")
    p.emit("vector", lambda v: v.memset(onesf[:, :], 1.0), inc="vc")
    p.emit("vector", lambda v: v.memset(onescol[:, :], 1.0), inc="vc")
    p.emit("vector", lambda v: v.memset(acc[:, :], 0.0), inc="vc")
    p.emit("vector", lambda v: v.memset(mexpb[:, :], -MEXP), inc="vc")
    p.emit("vector", lambda v: v.memset(keep[:, :], 0.0), inc="vc")

    for gi in range(NG):
        nch = nchpad_group(gi)
        p.dma(g, q1n[:, :].rearrange("p (h t) -> p h t", h=32),
              q1nfm_d.rearrange("(h p) t -> p h t", p=128)[:, :, gi * 256:(gi + 1) * 256],
              waits=[("tc", p.c("tc"))], inc="ldg")
        p.dma(g, q1p[:, :].rearrange("p (h t) -> p h t", h=32),
              q1pT_d.rearrange("(h p) t -> p h t", p=64)[:, :, gi * 256:(gi + 1) * 256],
              waits=[("tc", p.c("tc"))], inc="ldg")
        p.dma(g, iqg[:, :].rearrange("p (h t) -> p h t", h=16),
              iqfm_d.rearrange("(h p) t -> p h t", p=128)[:, :, gi * 256:(gi + 1) * 256],
              waits=[("tc", p.c("tc"))], inc="ldg")
        p.dma(g, iwg[:, :].rearrange("p (j n) -> p j n", j=2),
              iw_in.rearrange("(j p) n -> p j n", p=128)[:, 2 * gi:2 * gi + 2, :],
              waits=[("vc", p.c("vc"))], inc="ldg")

        for j2 in range(2):
            j = 2 * gi + j2
            nchj = nchpad_tile(j)
            kext = nchj * 128
            p.dma(g, cmask_sb[:, 0:kext], cmasks[j, :, 0:kext],
                  waits=[("vq", p.c("vq"))], inc="ldm")
            for n in range(16):
                nkb = (kext + 511) // 512
                for kb in range(nkb):
                    w_ = min(512, kext - kb * 512)
                    pa = ps[(n * nkb + kb) % 2]

                    def _mm(t, n=n, kb=kb, w_=w_, pa=pa, j2=j2):
                        return t.matmul(pa[:, 0:w_], iqg[:, n * 256 + j2 * 128: n * 256 + j2 * 128 + 128],
                                        ik_sb[:, kb * 512: kb * 512 + w_], start=True, stop=True)
                    p.emit("tensor", _mm, waits=[("ldg", p.c("ldg")), ("sq", p.c("sq") - 1),
                                                 ("vc", p.c("vc"))], inc="tc")
                    p.emit("scalar", lambda s, w_=w_, pa=pa: s.activation(
                        relub[:, 0:w_], pa[:, 0:w_], AF.Relu),
                        waits=[("tc", p.c("tc")), ("vq", p.c("vq"))], inc="sq")
                    if n == 0:
                        p.emit("vector", lambda v, kb=kb, w_=w_, j2=j2: v.tensor_scalar(
                            acc[:, kb * 512: kb * 512 + w_], relub[:, 0:w_],
                            iwg[:, j2 * 16: j2 * 16 + 1], None, OP.mult),
                            waits=[("sq", p.c("sq")), ("ldg", p.c("ldg"))], inc="vq")
                    else:
                        p.emit("vector", lambda v, n=n, kb=kb, w_=w_, j2=j2: v.scalar_tensor_tensor(
                            acc[:, kb * 512: kb * 512 + w_], relub[:, 0:w_],
                            iwg[:, j2 * 16 + n: j2 * 16 + n + 1],
                            acc[:, kb * 512: kb * 512 + w_], OP.mult, OP.add),
                            waits=[("sq", p.c("sq"))], inc="vq")
            p.emit("vector", lambda v, kext=kext: v.tensor_tensor(
                acc[:, 0:kext], acc[:, 0:kext], cmask_sb[:, 0:kext], OP.add),
                waits=[("ldm", p.c("ldm")), ("vq", p.c("vq"))], inc="vq")
            glob_T_min = 2 * 0 + 8 * (j // 2) + (j % 2)
            if glob_T_min >= 8 or True:
                p.emit("vector", lambda v, kext=kext: v.reduce_max(
                    bis[:, 1:2], acc[:, 0:kext], AX), inc="vq")
                p.emit("vector", lambda v, kext=kext: v.scalar_tensor_tensor(
                    negt[:, 0:kext], acc[:, 0:kext], -1.0, cmask_sb[:, 0:kext],
                    OP.mult, OP.add), waits=[("tc", p.c("tc"))], inc="vq")
                p.emit("vector", lambda v, kext=kext: v.reduce_max(
                    bis[:, 0:1], negt[:, 0:kext], AX), inc="vq")
                p.emit("vector", lambda v: v.tensor_scalar(
                    bis[:, 0:1], bis[:, 0:1], -1.0, None, OP.mult),
                    waits=[("vq", p.c("vq"))], inc="vq")
                for it in range(NBIS):
                    p.emit("vector", lambda v: v.tensor_scalar(
                        bis[:, 2:3], bis[:, 0:1], bis[:, 1:2], 0.5, OP.add, OP.mult),
                        waits=[("vq", p.c("vq"))], inc="vq")
                    p.emit("vector", lambda v, kext=kext: v.tensor_scalar(
                        pt[:, 0:kext], acc[:, 0:kext], bis[:, 2:3], None,
                        OP.is_ge, OP.add, accum_out=bis[:, 3:4]),
                        waits=[("vq", p.c("vq"))], inc="vq")
                    p.emit("vector", lambda v: v.tensor_scalar(
                        maski[:, :], bis[:, 3:4], 1024.0, None, OP.is_ge),
                        waits=[("vq", p.c("vq"))], inc="vq")
                    p.emit("vector", lambda v: v.select(
                        bis[:, 0:1], maski[:, :], bis[:, 2:3], bis[:, 0:1]),
                        waits=[("vq", p.c("vq"))], inc="vq")
                    p.emit("vector", lambda v: v.select(
                        bis[:, 1:2], maski[:, :], bis[:, 1:2], bis[:, 2:3]), inc="vq")
            if dbg_bis is not None:
                p.dma("sync", dbg_bis[j], bis[:, :], waits=[("vq", p.c("vq"))], inc="st")
            p.emit("vector", lambda v, j2=j2, kext=kext: v.tensor_scalar(
                keep[:, j2 * 4096: j2 * 4096 + kext], acc[:, 0:kext], bis[:, 0:1], None,
                OP.is_ge), waits=[("vq", p.c("vq"))], inc="vq")
            if nch * 128 > kext:
                p.emit("vector", lambda v, j2=j2, kext=kext, nch=nch: v.memset(
                    keep[:, j2 * 4096 + kext: j2 * 4096 + nch * 128], 0.0), inc="vq")
        if dbg_keep is not None:
            p.dma("sync", dbg_keep[gi], keep[:, :], waits=[("vq", p.c("vq"))], inc="st")
        for c in range(nch):
            for j2 in range(2):
                base = p.c("tvx")
                bank = (pst1, pst2)[base % 2]
                w = [("vq", p.c("vq"))]
                if c * 2 + j2 >= 2:
                    w.append(("tvx", base - 1))
                p.emit("tensor", lambda t, c=c, j2=j2, bank=bank: t.transpose(
                    bank[:, 0:128], keep[:, j2 * 4096 + c * 128: j2 * 4096 + (c + 1) * 128],
                    identb[:, :]), waits=w, inc="txp")
                p.emit("vector", lambda v, c=c, j2=j2, bank=bank: v.tensor_copy(
                    keepT[:, c * 256 + j2 * 128: c * 256 + j2 * 128 + 128], bank[:, 0:128]),
                    waits=[("txp", p.c("txp"))], inc="tvx")

        for h in range(32):
            kb0 = 0
            for g2 in range(nch // 2):
                m_, j_ = g2 % 4, g2 // 4
                p.dma("sync", kb_sb[:, g2 * 256:(g2 + 1) * 256],
                      ex_k1[m_ * 4096 + h * 128: m_ * 4096 + (h + 1) * 128,
                            256 * j_:256 * j_ + 256],
                      waits=[("tc", p.c("tc")), ("cck", 1)], inc="ldk")
                p.dma("sync", vb_sb[:, g2 * 256:(g2 + 1) * 256].rearrange(
                          "p (e v) -> p e v", e=2),
                      ex_v1[m_ * 1024 + 256 * j_: m_ * 1024 + 256 * j_ + 256,
                            h * 128:(h + 1) * 128].rearrange("(e p) v -> p e v", p=128),
                      waits=[("tc", p.c("tc")), ("ccv", 1)], inc="ldk")
            for kc in range(nch):
                pa = ps[kc % 2]

                def _mm(t, h=h, kc=kc, pa=pa, kb0=kb0):
                    t.matmul(pa[:, 0:256], kb_sb[:, kb0 + kc * 128: kb0 + kc * 128 + 128],
                             q1n[:, h * 256:(h + 1) * 256], start=True, stop=False)
                    return t.matmul(pa[:, 0:256], kpe_sb[:, kc * 128:(kc + 1) * 128],
                                    q1p[:, h * 256:(h + 1) * 256], start=False, stop=True)
                p.emit("tensor", _mm, waits=[("ldk", p.c("ldk")), ("ldg", p.c("ldg")),
                                             ("scA", p.c("scA") - 1)], inc="tc")
                p.emit("scalar", lambda s, kc=kc, pa=pa: s.activation(
                    pt[:, kc * 256:(kc + 1) * 256], pa[:, 0:256], AF.Exp,
                    bias=mexpb[:, 0:1], scale=SC1),
                    waits=[("tc", p.c("tc")), ("vc", 6)], inc="scA")
            p.emit("vector", lambda v, nch=nch: v.tensor_tensor(
                pt[:, 0:nch * 256], pt[:, 0:nch * 256], keepT[:, 0:nch * 256], OP.mult),
                waits=[("scA", p.c("scA")), ("tvx", p.c("tvx"))], inc="vs")
            def _sums(t, nch=nch):
                ins = None
                for kc in range(nch):
                    ins = t.matmul(psum1[0:1, 0:256], onescol[:, :],
                                   pt[:, kc * 256:(kc + 1) * 256],
                                   start=(kc == 0), stop=(kc == nch - 1))
                return ins
            p.emit("tensor", _sums, waits=[("vs", p.c("vs"))], inc="tc")
            p.emit("vector", lambda v: v.tensor_copy(rowsf[0:1, 0:256], psum1[0:1, 0:256]),
                   waits=[("tc", p.c("tc"))], inc="vs")
            if dbg_sums is not None:
                p.dma("sync", dbg_sums[gi, h], rowsf[0:1, 0:256], waits=[("vs", p.c("vs"))], inc="st")
            p.emit("vector", lambda v: v.reciprocal(rowsf[0:1, 256:512], rowsf[0:1, 0:256]),
                   waits=[("vs", p.c("vs"))], inc="vs")
            p.emit("tensor", lambda t: t.matmul(psm[:, 0:256], onesf[0:1, 0:128],
                                                rowsf[0:1, 256:512], start=True, stop=True),
                   waits=[("vs", p.c("vs"))], inc="tc")
            p.emit("vector", lambda v: v.tensor_copy(recipb[:, :], psm[:, 0:256]),
                   waits=[("tc", p.c("tc"))], inc="vs")
            pb = ps[2 + h % 2]

            def _pv(t, h=h, pb=pb, nch=nch, kb0=kb0):
                ins = None
                for kc in range(nch):
                    ins = t.matmul(pb[:, 0:256], vb_sb[:, kb0 + kc * 128: kb0 + kc * 128 + 128],
                                   pt[:, kc * 256:(kc + 1) * 256],
                                   start=(kc == 0), stop=(kc == nch - 1))
                return ins
            p.emit("tensor", _pv, waits=[("vs", p.c("vs")), ("vpv", p.c("vpv") - 1)], inc="tc")
            p.emit("vector", lambda v, h=h, pb=pb: v.tensor_tensor(
                attnfm[:, h * 256:(h + 1) * 256], pb[:, 0:256], recipb[:, :], OP.mult),
                waits=[("tc", p.c("tc"))], inc="vpv")

        if dbg_attn is not None:
            p.dma("sync", dbg_attn[gi], attnfm[:, :], waits=[("vpv", p.c("vpv"))], inc="st")
        for ob in range(16):
            p.dma(g, wbuf[:, :].rearrange("p (c n) -> p c n", c=NCD),
                  ow_d[:, ob * 256:(ob + 1) * 256].rearrange("(c p) n -> p c n", p=128),
                  waits=[("tc", p.c("tc"))], inc="ldw")
            for j2 in range(2):
                j = 2 * gi + j2
                pa = ps[j2 % 2]
                p.dma("sync", stage2[:, 0:256], x2part[j * 128:(j + 1) * 128, ob * 256:(ob + 1) * 256],
                      waits=[("vo", p.c("vo"))], inc="ldx")

                def _mm(t, j2=j2, pa=pa):
                    ins = None
                    for c in range(NCD):
                        ins = t.matmul(pa[:, 0:256],
                                       attnfm[:, c * 256 + j2 * 128: c * 256 + j2 * 128 + 128],
                                       wbuf[:, c * 256:(c + 1) * 256],
                                       start=(c == 0), stop=(c == NCD - 1))
                    return ins
                p.emit("tensor", _mm, waits=[("ldw", p.c("ldw")), ("vpv", p.c("vpv")),
                                             ("vo", p.c("vo"))], inc="tc")
                p.emit("vector", lambda v, pa=pa: v.scalar_tensor_tensor(
                    stage[:, 0:256], pa[:, 0:256], 1.0, stage2[:, 0:256], OP.mult, OP.add),
                    waits=[("tc", p.c("tc")), ("ldx", p.c("ldx")), ("stx", p.c("stx"))],
                    inc="vo")
                p.dma("sync", out_d[j * 128:(j + 1) * 128, ob * 256:(ob + 1) * 256],
                      stage[:, 0:256], waits=[("vo", p.c("vo"))], inc="stx")

    p.emit("sync", lambda e: e.nop(), waits=[("stx", p.c("stx")), ("st", p.c("st"))])


def build_fused():
    nc = new_nc()
    es = ExitStack()

    def din(n, sh, dt=F32):
        return nc.dram_tensor(n, sh, dt, kind="ExternalInput").ap()

    X = {}
    X["x0"] = din("x0", [TOK, D])
    X["wqkv"] = din("wqkv", [D, 3072])
    X["bqkv"] = din("bqkv", [1, 3072])
    X["identb_d"] = din("identb", [128, 128], BF16)
    for n, sh in [("wo", [1024, D]), ("bo", [1, D]), ("kern", [D, D]),
                  ("bias0", [1, D]), ("acomb", [D, 544]), ("bcomb", [544, D]),
                  ("gate_sel", [1, 128]), ("qa", [D, 1536]), ("qbn", [1536, 4096]),
                  ("qbp", [1536, 2048]), ("wq_row", [1, 1536]), ("kva", [D, 576]),
                  ("kvbk", [512, 4096]), ("kvbv", [512, 4096]), ("wkv_row", [1, 512]),
                  ("combow", [D, 32]), ("idxwq", [D, 2048]), ("idxwk", [D, 128]),
                  ("bthr", [1, 1]), ("wbaset", [D, D]), ("bbase", [1, D]),
                  ("acat", [D, 128]), ("bcat", [128, D]), ("ropec", [TOK, 32]),
                  ("ropes", [TOK, 32]), ("ow", [D, D])]:
        X[n] = din(n, sh)
    X["cmasks"] = din("cmasks", [NT, 128, 4096], BF16)

    ints = {}
    for n, sh, dt in [("xn0fm_d", [D, TOK], BF16), ("q0fm_d", [1024, TOK], BF16),
                      ("k0fm_d", [1024, TOK], BF16), ("v0tm_d", [TOK, 1024], BF16),
                      ("ex_k0", [4 * 1024, TOK], BF16), ("ex_v0", [4 * TOK, 1024], BF16),
                      ("x1part", [TOK, D], F32), ("x2p_d", [TOK, D], F32),
                      ("q1nfm_d", [4096, TOK], BF16), ("q1pe_d", [TOK, 2048], BF16),
                      ("q1pT_d", [2048, TOK], BF16), ("k1nfm_d", [4096, TOK], BF16),
                      ("k1pe_d", [TOK, 64], BF16), ("kpeT_d", [64, TOK], BF16),
                      ("v1tm_d", [TOK, 4096], BF16), ("ikfm_d", [128, TOK], BF16),
                      ("iqfm_d", [2048, TOK], BF16), ("iw_d", [TOK, 16], F32),
                      ("ex_k1", [4 * 4096, TOK], BF16), ("ex_v1", [4 * TOK, 4096], BF16),
                      ("ex_kpe", [4 * 64, TOK], BF16), ("ex_ik", [4 * 128, TOK], BF16)]:
        h = nc.dram_tensor(n, sh, dt, kind="Internal")
        ints[n] = h
        X[n] = h.ap()
    X["out_d"] = nc.dram_tensor("out", [TOK, D], F32, kind="ExternalOutput").ap()
    X["ints"] = ints

    X["ps"] = [es.enter_context(nc.psum_tensor(f"ps{i}", [128, 512], F32))
               for i in range(4)]
    X["pst1"] = es.enter_context(nc.psum_tensor("pst1", [128, 1024], BF16))
    X["pst2"] = es.enter_context(nc.psum_tensor("pst2", [128, 1024], BF16))
    X["psum1"] = es.enter_context(nc.psum_tensor("psum1", [128, 512], F32))
    X["psm"] = es.enter_context(nc.psum_tensor("psm", [128, 512], F32))

    mark = es.enter_context(nc.sbuf_tensor("arena_mark", [128, 32], F32))
    base = nc.lookup_mloc(mark).addr + 128

    p = Prog(nc, es)
    emit_phase1(nc, es, p, Arena(nc, "p1", base), X)
    emit_cc(p, ints["k0fm_d"], ints["ex_k0"], waits=[("st", p.c("st"))])
    emit_cc(p, ints["v0tm_d"], ints["ex_v0"], waits=[("st", p.c("st"))])
    barrier(p)
    emit_phase2(nc, es, p, Arena(nc, "p2", base), X)
    barrier(p)
    emit_phase3(nc, es, p, Arena(nc, "p3", base), X)
    if DBG_DUMP:
        snap = [(s, p.c(s)) for s in list(p.cnt.keys()) if p.c(s) > 0]
        for n in ["xn0fm_d", "q0fm_d", "k0fm_d", "v0tm_d", "ex_k0", "ex_v0",
                  "x1part", "x2p_d", "q1nfm_d", "q1pe_d", "q1pT_d", "k1nfm_d",
                  "k1pe_d", "kpeT_d", "v1tm_d", "ikfm_d", "iqfm_d", "iw_d",
                  "ex_k1", "ex_v1", "ex_kpe", "ex_ik"]:
            h = ints[n]
            dd = nc.dram_tensor("dbg_" + n, list(h.shape), h.dtype,
                                kind="ExternalOutput").ap()
            p.dma("sync", dd, h.ap(), waits=snap, inc="st")
        p.emit("sync", lambda e: e.nop(), waits=[("st", p.c("st"))])
    p.build()
    es.close()
    return nc


_NC_CACHE = {}


def _get_nc(name, builder):
    if name not in _NC_CACHE:
        _NC_CACHE[name] = builder()
    return _NC_CACHE[name]


def _own_sidx(m):
    u = np.arange(TOK)
    return 256 * (m + 4 * (u // 256)) + u % 256


def make_cmasks(m):
    cm = np.zeros((NT, 128, 4096), np.float32)
    for j in range(NT):
        T = 2 * m + 8 * (j // 2) + (j % 2)
        kext = nchpad_tile(j) * 128
        rows = 128 * T + np.arange(128)
        cols = np.arange(kext)
        cm[j, :, :kext] = np.where(cols[None, :] <= rows[:, None], 0.0, NEG)
        cm[j, :, kext:] = 0.0
    return cm.astype(bf)


def kernel(**inp):
    f32 = np.float32
    X_ = np.asarray(inp["L0_attn_x"], f32)
    identb = np.eye(128, dtype=bf)
    sidx = [_own_sidx(m) for m in range(4)]

    lA = np.asarray(inp["L0_ffn_lora_A"], f32)
    lB = np.asarray(inp["L0_ffn_lora_B"], f32)
    lAs = np.asarray(inp["L0_ffn_lora_A_shared"], f32)
    lBs = np.asarray(inp["L0_ffn_lora_B_shared"], f32)
    acomb = np.ascontiguousarray(np.concatenate(
        [lA.transpose(2, 0, 1).reshape(D, 512), lAs.T], axis=1))
    bcomb = np.ascontiguousarray(np.concatenate(
        [lB.transpose(0, 2, 1).reshape(512, D), lBs.T], axis=0))
    gate = np.asarray(inp["L0_ffn_gate"], f32)
    qb = np.asarray(inp["L1_attn_q_b_w"], f32).reshape(1536, 32, 192)
    qbn = np.ascontiguousarray(qb[:, :, :128].reshape(1536, 4096))
    qbp = np.ascontiguousarray(qb[:, :, 128:].reshape(1536, 2048))
    kvb = np.asarray(inp["L1_attn_kv_b_w"], f32).reshape(512, 32, 256)
    kvbk = np.ascontiguousarray(kvb[:, :, :128].reshape(512, 4096))
    kvbv = np.ascontiguousarray(kvb[:, :, 128:].reshape(512, 4096))
    combow = np.zeros((D, 32), f32)
    combow[:, 0:8] = np.asarray(inp["L1_ffn_W_gate"], f32).T
    combow[:, 8] = np.asarray(inp["L1_ffn_W_thr"], f32)
    combow[:, 9:25] = np.asarray(inp["L1_attn_idx_weights_w"], f32)
    acat = np.ascontiguousarray(
        np.asarray(inp["L1_ffn_A_w"], f32).transpose(2, 0, 1).reshape(D, 128))
    bcat = np.ascontiguousarray(
        np.asarray(inp["L1_ffn_B_w"], f32).transpose(0, 2, 1).reshape(128, D))
    inv = 1.0 / 10000.0 ** (np.arange(0, 64, 2, dtype=np.float64) / 64)
    posf = np.arange(4096, dtype=np.float64)
    ang = posf[:, None] * inv[None, :]
    cosT, sinT = np.cos(ang).astype(f32), np.sin(ang).astype(f32)

    nc = _get_nc("fused", build_fused)
    maps = []
    for c in range(8):
        b, m = c // 4, c % 4
        gs = np.zeros((1, 128), f32)
        for j in range(NT):
            gs[0, j * 16:(j + 1) * 16] = gate[b * 4 + (j // 2)]
        maps.append({
            "x0": np.ascontiguousarray(X_[b, sidx[m], :]),
            "wqkv": np.asarray(inp["L0_attn_Wqkv"], f32),
            "bqkv": np.asarray(inp["L0_attn_bqkv"], f32).reshape(1, 3072),
            "identb": identb,
            "wo": np.asarray(inp["L0_attn_Wo"], f32),
            "bo": np.asarray(inp["L0_attn_bo"], f32).reshape(1, D),
            "kern": np.asarray(inp["L0_ffn_kernel"], f32),
            "bias0": np.asarray(inp["L0_ffn_bias"], f32).reshape(1, D),
            "acomb": acomb, "bcomb": bcomb, "gate_sel": gs,
            "qa": np.asarray(inp["L1_attn_q_a_w"], f32),
            "qbn": qbn, "qbp": qbp,
            "wq_row": np.asarray(inp["L1_attn_q_a_norm_w"], f32).reshape(1, 1536),
            "kva": np.asarray(inp["L1_attn_kv_a_w"], f32),
            "kvbk": kvbk, "kvbv": kvbv,
            "wkv_row": np.asarray(inp["L1_attn_kv_a_norm_w"], f32).reshape(1, 512),
            "combow": combow,
            "idxwq": np.asarray(inp["L1_attn_idx_wq_w"], f32),
            "idxwk": np.asarray(inp["L1_attn_idx_wk_w"], f32),
            "bthr": np.asarray(inp["L1_ffn_b_thr"], f32).reshape(1, 1),
            "wbaset": np.ascontiguousarray(np.asarray(inp["L1_ffn_W_base"], f32).T),
            "bbase": np.asarray(inp["L1_ffn_b_base"], f32).reshape(1, D),
            "acat": acat, "bcat": bcat,
            "ropec": np.ascontiguousarray(cosT[sidx[m], :]),
            "ropes": np.ascontiguousarray(sinT[sidx[m], :]),
            "cmasks": make_cmasks(m),
            "ow": np.asarray(inp["L1_attn_o_w"], f32),
        })
    r = run_bass_kernel_spmd(nc, maps, core_ids=list(range(8))).results

    out = np.empty((2, 4096, D), f32)
    for c in range(8):
        b, m = c // 4, c % 4
        out[b, sidx[m], :] = np.asarray(r[c]["out"], f32)
    return out
